# version 05
# baseline (speedup 1.0000x reference)
import numpy as np
import ml_dtypes
from contextlib import ExitStack
import concourse.bass as bass
import concourse.mybir as mybir
from concourse.bass_utils import run_bass_kernel_spmd

F32 = mybir.dt.float32
BF16 = mybir.dt.bfloat16
AF = mybir.ActivationFunctionType
ALU = mybir.AluOpType
AX = mybir.AxisListType
BF = ml_dtypes.bfloat16

B, S, H = 4, 2048, 2048
T = 1024
ROWS = 2560
NROWT = 20
NC = 16
NH0 = 16
WIN, CAP = 512, 50.0
NI0 = 44
NI1 = 64
SC0 = float(1.0 / np.sqrt(128.0))
SC1 = 0.0
EPS_LN = 1e-5
EPS_RMS = 1e-6


def _stage_kn(w, kcg):
    K, N = w.shape
    ng = K // (128 * kcg)
    return np.ascontiguousarray(w.reshape(ng, kcg, 128, N).transpose(0, 2, 1, 3))


def _stage_colblocks(w, cbw, kcg):
    K, N = w.shape
    ncb = N // cbw
    ng = K // (128 * kcg)
    r = w.reshape(ng, kcg, 128, ncb, cbw)
    return np.ascontiguousarray(r.transpose(3, 0, 2, 1, 4))


def _stage_sweeps(w, nsw, kcg):
    K, N = w.shape
    sw = N // nsw
    ng = K // (128 * kcg)
    r = w.reshape(ng, kcg, 128, nsw, sw)
    return np.ascontiguousarray(r.transpose(3, 0, 2, 1, 4))


def _perm24():
    g = np.arange(8192 // 4)
    j = np.arange(4)
    newpos = (4 * (g[:, None] // 128) + j[None, :]) * 128 + (g[:, None] % 128)
    perm = np.empty(8192, dtype=np.int64)
    perm[newpos.reshape(-1)] = np.arange(8192)
    return perm


def stage_inputs(inp):
    x = np.asarray(inp["L0_attn_h"], dtype=np.float32)
    f32 = lambda k: np.asarray(inp[k], np.float32)

    wqc = _stage_kn(f32("L0_attn_Wq_c"), 4)
    wqup = _stage_kn(f32("L0_attn_Wq_up"), 4)
    wkv0 = _stage_kn(np.ascontiguousarray(
        np.concatenate([f32("L0_attn_Wk"), f32("L0_attn_Wv")], axis=1)), 16)
    wcomp = _stage_kn(f32("L0_attn_W_comp"), 16)
    wg = np.ascontiguousarray(f32("L0_attn_Wg").reshape(4, 4, 128, 2048))
    wo0 = _stage_sweeps(f32("L0_attn_Wo"), 4, 4)
    wgate0 = _stage_colblocks(np.ascontiguousarray(f32("L0_ffn_gate_w").T), 256, 4)
    wup0 = _stage_colblocks(np.ascontiguousarray(f32("L0_ffn_up_w").T), 256, 4)
    wdown0 = _stage_sweeps(np.ascontiguousarray(f32("L0_ffn_down_w").T), 4, 4)
    wq1 = _stage_colblocks(f32("L1_attn_Wq"), 256, 4)
    wk1 = _stage_kn(f32("L1_attn_Wk"), 1)
    wv1 = _stage_kn(f32("L1_attn_Wv"), 1)
    wo1 = _stage_colblocks(f32("L1_attn_Wo"), 256, 4)
    perm = _perm24()
    wgate1 = _stage_colblocks(np.ascontiguousarray(f32("L1_ffn_w_gate")[:, perm]), 256, 4)
    wup1 = _stage_colblocks(np.ascontiguousarray(f32("L1_ffn_w_up")[:, perm]), 256, 4)
    wdown1 = _stage_sweeps(np.ascontiguousarray(f32("L1_ffn_w_down")[perm, :]), 4, 4)

    pp = lambda k, n: np.ascontiguousarray(f32(k).reshape(n, 128).T)
    shared = dict(
        wqc=wqc, wqup=wqup, wkv0=wkv0, wcomp=wcomp, wg=wg, wo0=wo0,
        wgate0=wgate0, wup0=wup0, wdown0=wdown0, wq1=wq1, wk1=wk1, wv1=wv1,
        wo1=wo1, wgate1=wgate1, wup1=wup1, wdown1=wdown1,
        gs0=pp("L0_ffn_gate_s", NI0), us0=pp("L0_ffn_up_s", NI0),
        ds0=pp("L0_ffn_down_s", NC), bo1=pp("L1_attn_bo", NC),
        qnw=f32("L0_attn_q_norm_w").reshape(1, 128),
        knw=f32("L0_attn_k_norm_w").reshape(1, 128),
        sink=f32("L0_attn_sink").reshape(1, 16))

    invf = 1.0 / 10000.0 ** (np.arange(0, 64, 2, dtype=np.float64) / 64.0)
    in_maps = []
    for c in range(8):
        seq, half = c // 2, c % 2
        qstart = half * T
        rows = (qstart - 512 + np.arange(ROWS)) % S
        xall = np.ascontiguousarray(x[seq, rows, :].reshape(NROWT, 128, H))
        x0f = np.ascontiguousarray(
            x[seq, qstart:qstart + T, :].T.reshape(NC, 128, T))

        ang = rows[:1536].astype(np.float64)[:, None] * invf[None, :]
        cos = np.cos(ang).astype(np.float32).reshape(12, 128, 32)
        sin = np.sin(ang).astype(np.float32).reshape(12, 128, 32)

        qg = qstart + np.arange(T)
        kraw = (qstart - 512) + np.arange(1536)
        msw = np.zeros((8, 128, 641), dtype=BF)
        msc = np.zeros((8, 128, 129), dtype=BF)
        blk_g = ((qstart + 16 * np.arange(128)) % S) // 16
        for qt in range(8):
            q_abs = qg[qt * 128:(qt + 1) * 128]
            k_abs = kraw[qt * 128: qt * 128 + 640]
            valid = (k_abs[None, :] >= 0) & (k_abs[None, :] <= q_abs[:, None]) \
                & (q_abs[:, None] - k_abs[None, :] < WIN)
            msw[qt, :, :640] = valid.astype(BF)
            msw[qt, :, 640] = BF(1.0)
            cv = (16 * blk_g[None, :] + 15) < q_abs[:, None]
            msc[qt, :, :128] = cv.astype(BF)
            msc[qt, :, 128] = BF(1.0)
        m = dict(shared)
        m.update(xall=xall, x0f=x0f, cos=np.ascontiguousarray(cos),
                 sin=np.ascontiguousarray(sin), msw=msw, msc=msc)
        in_maps.append(m)
    return in_maps



class Sem:
    __slots__ = ("k", "h", "off", "n")

    def __init__(self, k, h, off):
        self.k, self.h, self.off, self.n = k, h, off, 0

    def inc(self, instr, c=1):
        instr.then_inc(self.h, c)
        self.n += c
        return instr

    def wait(self, eng, v):
        eng.wait_ge(self.h, self.off + v)

    def release(self):
        self.k.free_sems.append((self.h, self.off + self.n))


class K:
    def __init__(self, dbg=()):
        self.dbg = set(dbg)
        self.nc = bass.Bass("TRN2", target_bir_lowering=False)
        self.st = ExitStack()
        self.d = {}
        self.dbg_outs = {}
        self.final_sems = []
        self.free_sems = []
        self.sem_ring = [[], [], []]
        self.keep_sems = []

    def sem(self, name, keep=False, dma=False):
        if dma or keep:
            s = Sem(self, self.nc.alloc_semaphore(name=name), 0)
            self.keep_sems.append(s)
            return s
        if self.free_sems:
            h, off = self.free_sems.pop()
        else:
            h, off = self.nc.alloc_semaphore(name=name), 0
        s = Sem(self, h, off)
        self.sem_ring[0].append(s)
        return s

    def phase(self):
        for s_ in self.sem_ring.pop():
            s_.release()
        self.sem_ring.insert(0, [])

    def sb(self, name, shape, dt=F32):
        return self.st.enter_context(
            self.nc.sbuf_tensor(name, list(shape), dt, side="left"))

    def rsb(self, stack, name, shape, dt=F32):
        return stack.enter_context(
            self.nc.sbuf_tensor(name, list(shape), dt, side="right"))

    def psum(self, name, shape):
        return self.st.enter_context(self.nc.psum_tensor(name, list(shape), F32))

    def at(self, name, shape, dt, off_kb):
        return self.nc.alloc_sbuf_tensor_at(
            name, list(shape), dt, offset=self.arena_base + int(off_kb * 1024))

    def din(self, name, shape, dt=F32):
        self.d[name] = self.nc.dram_tensor(name, list(shape), dt,
                                           kind="ExternalInput").ap()

    def mm(self, t, out, lhsT, rhs, start, stop, ldw=True):
        if ldw:
            return t.matmul(out, lhsT, rhs, start=start, stop=stop)
        nc = self.nc
        with nc.discard():
            tmp = t.matmul(out, lhsT, rhs, start=start, stop=stop).ins
        new = mybir.InstMatmult(
            name=nc.get_next_instruction_name(), engine=tmp.engine,
            ins=list(tmp.ins), outs=list(tmp.outs),
            replication_resolution=tmp.replication_resolution,
            replication_shift_amnt=tmp.replication_shift_amnt,
            replication_num_rows=tmp.replication_num_rows,
            start_tensor_calc=tmp.start_tensor_calc,
            stop_tensor_calc=tmp.stop_tensor_calc,
            ldweights=False)
        return t.add_instruction(new)

    def dbg_dump(self, name, src_list, ready_sem, shape, dt):
        if name not in self.dbg:
            return
        o = self.nc.dram_tensor("dbg_" + name, list(shape), dt,
                                kind="ExternalOutput").ap()
        self.dbg_outs[name] = o
        s_d = self.sem("dbgs_" + name)
        need = ready_sem.n

        @self.blk.sync
        def _(sy):
            ready_sem.wait(sy, need)
            off = 0
            for src in src_list:
                n = int(np.prod(src.shape))
                s_d.inc(sy.dma_start(
                    out=o[:, off:off + n // src.shape[0]] if len(src_list) > 1 else o,
                    in_=src), 16)
                off += n // src.shape[0]
        self.final_sems.append(s_d)

    def declare_io(self):
        din = self.din
        din("xall", (NROWT, 128, H)); din("x0f", (NC, 128, T))
        din("cos", (12, 128, 32)); din("sin", (12, 128, 32))
        din("msw", (8, 128, 641), BF16); din("msc", (8, 128, 129), BF16)
        din("wqc", (4, 128, 4, 512)); din("wqup", (1, 128, 4, 2048))
        din("wkv0", (1, 128, 16, 256)); din("wcomp", (16, 128, 16, 128))
        din("wg", (4, 4, 128, 2048)); din("wo0", (4, 16, 128, 4, 512))
        din("wgate0", (22, 4, 128, 4, 256)); din("wup0", (22, 4, 128, 4, 256))
        din("wdown0", (4, 11, 128, 4, 512))
        din("wq1", (8, 4, 128, 4, 256))
        din("wk1", (16, 128, 1, 2048)); din("wv1", (16, 128, 1, 2048))
        din("wo1", (8, 4, 128, 4, 256))
        din("wgate1", (32, 4, 128, 4, 256)); din("wup1", (32, 4, 128, 4, 256))
        din("wdown1", (4, 16, 128, 4, 512))
        din("gs0", (128, NI0)); din("us0", (128, NI0)); din("ds0", (128, NC))
        din("bo1", (128, NC)); din("qnw", (1, 128)); din("knw", (1, 128))
        din("sink", (1, 16))

        nc = self.nc
        self.out_d = nc.dram_tensor("out", [NC, 128, T], F32, kind="ExternalOutput").ap()
        self.inter_s = nc.dram_tensor("inter_s", [64, 128, T], BF16).ap()
        self.x1_s = nc.dram_tensor("x1_s", [NC, 128, T], F32).ap()
        self.s1_s = nc.dram_tensor("s1_s", [NC, 128, T], BF16).ap()
        self.cc_in = nc.dram_tensor("cc_in", [128, 16 * 128 + 16], F32)
        self.cc_out = nc.dram_tensor("cc_out", [128, 16 * 128 + 16], F32)

        self.ident_d = nc.inline_tensor(np.eye(128, dtype=BF), "identc")
        e2 = np.zeros((2, 128), dtype=BF); e2[0, :64] = 1; e2[1, 64:] = 1
        self.e2_d = nc.inline_tensor(e2, "e2c")
        cvals = np.array([EPS_LN, EPS_RMS, 1e-6], dtype=np.float32)
        self.consts_d = nc.inline_tensor(
            np.ascontiguousarray(np.broadcast_to(cvals, (128, 3))), "constsc")
        onescol = np.ones((128, 1), dtype=BF)
        self.onesc_d = nc.inline_tensor(onescol, "onescol")
        onesrow = np.ones((1, 128), dtype=BF)
        self.onesr_d = nc.inline_tensor(onesrow, "onesrow")
        self.onesrF_d = nc.inline_tensor(np.ones((1, 128), dtype=np.float32), "onesrowF")
        e2f = np.zeros((2, 128), dtype=np.float32); e2f[0, :64] = 1; e2f[1, 64:] = 1
        self.e2F_d = nc.inline_tensor(e2f, "e2F")

    def alloc_persistent(self):
        sb = self.sb
        self.ident_sb = sb("ident_sb", (128, 128), BF16)
        self.onesc_sb = sb("onesc_sb", (128, 1), BF16)
        self.onesr_sb = sb("onesr_sb", (1, 128), BF16)
        self.e2_sb = sb("e2_sb", (2, 128), BF16)
        self.onesrF_sb = sb("onesrF_sb", (1, 128), F32)
        self.e2F_sb = sb("e2F_sb", (2, 128), F32)
        self.const_sb = sb("const_sb", (128, 3), F32)
        self.cos_sb = sb("cos_sb", (128, 12 * 32), F32)
        self.sin_sb = sb("sin_sb", (128, 12 * 32), F32)
        self.msw_sb = sb("msw_sb", (128, 8 * 641), BF16)
        self.msc_sb = sb("msc_sb", (128, 8 * 129), BF16)
        self.sink_sb = sb("sink_sb", (128, 16), F32)
        self.qnwb_sb = sb("qnwb_sb", (128, 128), F32)
        self.knwb_sb = sb("knwb_sb", (128, 128), F32)
        self.negm_sb = sb("negm_sb", (128, 16), F32)
        self.negsink_sb = sb("negsink_sb", (128, 16), F32)
        self.stsw_sb = sb("stsw_sb", (128, 16), F32)
        self.sksc_sb = sb("sksc_sb", (128, 16), F32)
        self.gs0_sb = sb("gs0_sb", (128, NI0), F32)
        self.us0_sb = sb("us0_sb", (128, NI0), F32)
        self.ds0_sb = sb("ds0_sb", (128, NC), F32)
        self.bo1_sb = sb("bo1_sb", (128, NC), F32)
        self.arena_res = sb("arena_res", (128, 160 * 1024), mybir.dt.uint8)
        self.arena_base = self.nc.lookup_mloc(self.arena_res).addr

    def p0(self):
        self.phase()
        blk, d = self.blk, self.d
        self.s_p0 = self.sem("p0_ld", dma=True)
        s_p0 = self.s_p0

        @blk.sync
        def _(sy):
            pairs = [
                (self.ident_sb[:, :], self.ident_d.ap()),
                (self.onesc_sb[:, :], self.onesc_d.ap()),
                (self.onesr_sb[:, :], self.onesr_d.ap()),
                (self.e2_sb[:, :], self.e2_d.ap()),
                (self.onesrF_sb[:, :], self.onesrF_d.ap()),
                (self.e2F_sb[:, :], self.e2F_d.ap()),
                (self.const_sb[:, :], self.consts_d.ap()),
                (self.cos_sb[:, :].rearrange("p (t j) -> p t j", j=32),
                 d["cos"].transpose([1, 0, 2])),
                (self.sin_sb[:, :].rearrange("p (t j) -> p t j", j=32),
                 d["sin"].transpose([1, 0, 2])),
                (self.msw_sb[:, :].rearrange("p (q j) -> p q j", j=641),
                 d["msw"].transpose([1, 0, 2])),
                (self.msc_sb[:, :].rearrange("p (q j) -> p q j", j=129),
                 d["msc"].transpose([1, 0, 2])),
                (self.sink_sb[:, :], d["sink"].broadcast_to([128, 16])),
                (self.qnwb_sb[:, :], d["qnw"].broadcast_to([128, 128])),
                (self.knwb_sb[:, :], d["knw"].broadcast_to([128, 128])),
                (self.gs0_sb[:, :], d["gs0"]), (self.us0_sb[:, :], d["us0"]),
                (self.ds0_sb[:, :], d["ds0"]), (self.bo1_sb[:, :], d["bo1"]),
            ]
            for dst, src in pairs:
                s_p0.inc(sy.dma_start(out=dst, in_=src), 16)

        self.s_p0v = self.sem("p0_v")

        @blk.vector
        def _(v):
            s_p0.wait(v, s_p0.n)
            v.tensor_scalar(self.negm_sb[:, :], self.sink_sb[:, :], 50.0, -1.0,
                            ALU.max, ALU.mult)
            v.tensor_scalar(self.sksc_sb[:, :], self.sink_sb[:, :],
                            float(1.0 / SC0), None, ALU.mult)
            v.tensor_scalar(self.negsink_sb[:, :], self.sink_sb[:, :], -1.0, None,
                            ALU.mult)
            v.drain()
            self.s_p0v.inc(v.tensor_add(self.stsw_sb[:, :], self.sink_sb[:, :],
                                        self.negm_sb[:, :]))

        self.s_p0s = self.sem("p0_s")

        @blk.scalar
        def _(s):
            self.s_p0v.wait(s, self.s_p0v.n)
            self.s_p0s.inc(s.activation(self.stsw_sb[:, :], self.stsw_sb[:, :], AF.Exp))

    def p1(self):
        self.phase()
        blk, d, nc = self.blk, self.d, self.nc
        sb, psum, sem = self.sb, self.psum, self.sem
        ph = ExitStack()
        psb = lambda n, s, dt=F32: self.rsb(ph, n, s, dt)
        self.xnB = self.at("xnB", (128, NC * 1024), BF16, 0)
        self.xnA = self.at("xnA", (128, NC * 512), BF16, 32)
        self.xnC = self.at("xnC", (128, NC * 1024), BF16, 48)
        xsl = [psb(f"xsl{i}", (128, H)) for i in range(3)]
        xnt = [self.at(f"xnt{i}", (128, H), BF16, 80 + 4 * i) for i in range(2)]
        lnsc = psb("lnsc", (128, 8))
        sqjunk = self.at("sqjunk", (128, H), F32, 88)
        self.stTR = ExitStack()
        self.ps_tr = [self.stTR.enter_context(
            nc.psum_tensor(f"pstr{i}", [128, 512], F32)) for i in range(2)]
        ps_tr = self.ps_tr

        s_ld = [sem(f"p1ld{i}", dma=True) for i in range(3)]
        s_ss = sem("p1ss")
        s_sv = sem("p1sv")
        s_ap = sem("p1ap")
        s_tm = sem("p1tm")
        s_te = sem("p1te")

        @blk.sync
        def _(sy):
            for i in range(NROWT):
                sl = i % 3
                if i >= 3:
                    s_ap.wait(sy, i - 2)
                s_ld[sl].inc(sy.dma_start(out=xsl[sl][:, :], in_=d["xall"][i]), 16)

        @blk.scalar
        def _(s):
            for i in range(NROWT):
                sl = i % 3
                s_ld[sl].wait(s, 16 * (i // 3 + 1))
                if i:
                    s.drain()
                s_ss.inc(s.activation(sqjunk[:, :], xsl[sl][:, :], AF.Square,
                                      accum_out=lnsc[:, 2:3]))
                s_sv.wait(s, i + 1)
                s_ss.inc(s.activation(lnsc[:, 4:5], lnsc[:, 3:4], AF.Sqrt,
                                      bias=self.const_sb[:, 0:1]))

        def xn_target(i, fc):
            if i < 4:
                return self.xnA[:, fc * 512 + i * 128: fc * 512 + (i + 1) * 128]
            if i < 12:
                j = i - 4
                return self.xnB[:, fc * 1024 + j * 128: fc * 1024 + (j + 1) * 128]
            j = i - 12
            return self.xnC[:, fc * 1024 + j * 128: fc * 1024 + (j + 1) * 128]


        def evac_tile(v, i):
            for g4 in range(4):
                gi = i * 4 + g4
                s_tm.wait(v, gi * 4 + 4)
                last = None
                for j in range(4):
                    last = v.tensor_copy(xn_target(i, g4 * 4 + j),
                                         ps_tr[gi % 2][:, j * 128:(j + 1) * 128])
                s_te.inc(last)

        @blk.vector
        def _(v):
            for i in range(NROWT):
                if i >= 1:
                    evac_tile(v, i - 1)
                sl = i % 3
                s_ld[sl].wait(v, 16 * (i // 3 + 1))
                if i:
                    v.drain()
                v.reduce_sum(lnsc[:, 0:1], xsl[sl][:, :], axis=AX.X)
                v.drain()
                v.tensor_scalar(lnsc[:, 0:1], lnsc[:, 0:1], 1.0 / H, None, ALU.mult)
                v.drain()
                v.tensor_mul(lnsc[:, 1:2], lnsc[:, 0:1], lnsc[:, 0:1])
                s_ss.wait(v, 2 * i + 1)
                v.tensor_scalar(lnsc[:, 3:4], lnsc[:, 2:3], 1.0 / H, None, ALU.mult)
                v.drain()
                s_sv.inc(v.tensor_sub(lnsc[:, 3:4], lnsc[:, 3:4], lnsc[:, 1:2]))
                s_ss.wait(v, 2 * i + 2)
                v.reciprocal(lnsc[:, 5:6], lnsc[:, 4:5])
                v.drain()
                v.tensor_mul(lnsc[:, 6:7], lnsc[:, 5:6], lnsc[:, 5:6])
                v.drain()
                v.tensor_mul(lnsc[:, 6:7], lnsc[:, 6:7], lnsc[:, 3:4])
                v.drain()
                v.tensor_scalar(lnsc[:, 6:7], lnsc[:, 6:7], -0.5, 1.5,
                                ALU.mult, ALU.add)
                v.drain()
                v.tensor_mul(lnsc[:, 5:6], lnsc[:, 5:6], lnsc[:, 6:7])
                v.drain()
                ts_ = i % 2
                if i >= 2:
                    s_tm.wait(v, 16 * (i - 1))
                s_ap.inc(v.tensor_scalar(xnt[ts_][:, :], xsl[sl][:, :],
                                         lnsc[:, 0:1], lnsc[:, 5:6],
                                         ALU.subtract, ALU.mult))
            evac_tile(v, NROWT - 1)

        @blk.tensor
        def _(t):
            self.s_p0.wait(t, self.s_p0.n)
            for i in range(NROWT):
                ts_ = i % 2
                s_ap.wait(t, i + 1)
                for g4 in range(4):
                    gi = i * 4 + g4
                    if gi >= 2:
                        s_te.wait(t, gi - 1)
                    for j in range(4):
                        fc = g4 * 4 + j
                        s_tm.inc(self.mm(t, ps_tr[gi % 2][:, j * 128:(j + 1) * 128],
                                         xnt[ts_][:, fc * 128:(fc + 1) * 128],
                                         self.ident_sb[:, :], True, True))


        self.s_p1_done = s_te
        self.p1_gates = [(s_ap, s_ap.n), (s_ss, s_ss.n), (s_tm, s_tm.n),
                         (s_te, s_te.n)]
        self.dbg_dump3("xn", [("xnA", self.xnA), ("xnB", self.xnB),
                              ("xnC", self.xnC)], s_te, BF16)
        ph.close()

    def dbg_dump3(self, name, tensors, ready, dt):
        if name not in self.dbg:
            return
        need = ready.n
        for tn, tsb in tensors:
            o = self.nc.dram_tensor("dbg_" + tn, list(tsb.shape), dt,
                                    kind="ExternalOutput").ap()
            self.dbg_outs[tn] = o
            s_d = self.sem("dbgs_" + tn, keep=True)

            def mk(o=o, tsb=tsb, s_d=s_d):
                @self.blk.sync
                def _(sy):
                    ready.wait(sy, need)
                    s_d.inc(sy.dma_start(out=o, in_=tsb[:, :]), 16)

                @self.blk.vector
                def _(v):
                    s_d.wait(v, 16)

                @self.blk.scalar
                def _(s):
                    s_d.wait(s, 16)

                @self.blk.gpsimd
                def _(g):
                    s_d.wait(g, 16)
            mk()
            self.final_sems.append(s_d)


    def p2(self):
        self.phase()
        blk, d, nc = self.blk, self.d, self.nc
        sem = self.sem
        ph = ExitStack()
        psb = lambda n, s, dt=F32: self.rsb(ph, n, s, dt)
        pps = lambda n, s: ph.enter_context(nc.psum_tensor(n, list(s), F32))

        self.q_fm = self.at("q_fm", (128, 16 * 1024), BF16, 80)
        self.k_fm = self.at("k_fm", (128, 1536), BF16, 112)
        self.v_tm = self.at("v_tm", (128, 1536), BF16, 115)
        self.ck_tm = self.at("ck_tm", (128, 128), BF16, 118)
        self.ck_fm = self.at("ck_fm", (128, 128), BF16, 118.5)

        wqc_sb = [psb(f"wqc_sb{i}", (128, 2048), BF16) for i in range(2)]
        wqup_sb = self.at("wqup_sb", (128, 4 * 2048), BF16, 120)
        wkv_sb = psb("wkv_sb", (128, 16 * 256), BF16)
        wcp_sb = [psb(f"wcp{i}", (128, 16 * 128), BF16) for i in range(2)]
        z_sb = self.at("z_sb", (128, 4 * 1024), BF16, 136)
        qrot = [self.at(f"qrot{i}", (128, 1024), F32, 144 + 4 * i) for i in range(2)]
        qsq = self.at("qsq", (128, 1024), BF16, 156)
        qtmp = [psb(f"qtmp{i}", (128, 256), F32) for i in range(2)]
        qst = psb("qst", (128, 2 * 4 * 8), F32)
        qn_sb = [self.at(f"qn{i}", (128, 1024), BF16, 152 + 2 * i) for i in range(2)]
        krot = psb("krot", (128, 128), F32)
        ktmp = psb("ktmp", (128, 64), F32)
        ksq = psb("ksq", (128, 128), F32)
        kst = psb("kst", (128, 4), F32)
        kn_sb = psb("kn_sb", (128, 128), BF16)
        cr_sb = psb("cr_sb", (128, 128), BF16)
        ctf = psb("ctf", (128, 128), F32)
        csq = psb("csq", (128, 128), F32)

        ps_zq = [pps(f"ps_zq{i}", (128, 1024)) for i in range(2)]
        ps_kv = [pps(f"ps_kv{i}", (128, 256)) for i in range(2)]
        ps_t2 = self.ps_tr

        s_wld = sem("p2wld", dma=True)
        s_wqc = [sem(f"p2wqc{i}", dma=True) for i in range(2)]
        s_wqcd = sem("p2wqcd")
        s_cpl = [sem(f"p2cp{i}", dma=True) for i in range(2)]
        s_zmm = sem("p2zmm"); s_zev = sem("p2zev")
        s_qmm = sem("p2qmm"); s_qps = sem("p2qps")
        s_qrs = sem("p2qrs"); s_qsr = sem("p2qsr")
        s_qn = sem("p2qn"); s_qtr = sem("p2qtr"); s_qte = sem("p2qte")
        s_kmm = sem("p2kmm"); s_kps = sem("p2kps")
        s_krs = sem("p2krs"); s_ksr = sem("p2ksr")
        s_kn = sem("p2kn"); s_ktr = sem("p2ktr"); s_kte = sem("p2kte")
        s_ckmm = sem("p2ckmm"); s_ckv = sem("p2ckv"); s_cktr = sem("p2cktr")

        gate = self.s_p1_done
        GATE = gate.n

        @blk.gpsimd
        def _(g):
            for gs, gn in self.p1_gates:
                gs.wait(g, gn)
            s_wld.inc(g.dma_start(out=wqup_sb[:, :],
                                  in_=d["wqup"][0].rearrange("p k c -> p (k c)")), 16)
            s_wld.inc(g.dma_start(out=wkv_sb[:, :],
                                  in_=d["wkv0"][0].rearrange("p k c -> p (k c)")), 16)
            for it in range(8):
                kcg = it % 4
                if it >= 2:
                    s_wqcd.wait(g, it - 1)
                s_wqc[it % 2].inc(g.dma_start(
                    out=wqc_sb[it % 2][:, :],
                    in_=d["wqc"][kcg].rearrange("p k c -> p (k c)")), 16)
            for i in range(16):
                if i >= 2:
                    s_ckmm.wait(g, 16 * (i - 1))
                s_cpl[i % 2].inc(g.dma_start(
                    out=wcp_sb[i % 2][:, :],
                    in_=d["wcomp"][i].rearrange("p k c -> p (k c)")), 16)

        def v_rsqrt(v, ss, srt, r0, msq):
            v.reciprocal(r0, srt)
            v.tensor_scalar(msq, ss, 1.0 / 128.0, self.const_sb[:, 1:2],
                            ALU.mult, ALU.add)
            v.drain()
            v.tensor_mul(msq, msq, r0)
            v.drain()
            v.tensor_mul(msq, msq, r0)
            v.drain()
            v.tensor_scalar(msq, msq, -0.5, 1.5, ALU.mult, ALU.add)
            v.drain()
            v.tensor_mul(r0, r0, msq)
            v.drain()

        @blk.tensor
        def _(t):
            gate.wait(t, GATE)
            s_wld.wait(t, 32)
            for ts in range(2):
                for it4 in range(4):
                    it = ts * 4 + it4
                    kcg = it4
                    s_wqc[it % 2].wait(t, 16 * (it // 2 + 1))
                    for m in range(4):
                        for kcs in range(4):
                            kc = kcg * 4 + kcs
                            lhsT = wqc_sb[it % 2][:, kcs * 512 + m * 128:
                                                  kcs * 512 + (m + 1) * 128]
                            i_ = self.mm(t, ps_zq[m // 2][:, (m % 2) * 512:
                                                          (m % 2 + 1) * 512], lhsT,
                                         self.xnB[:, kc * 1024 + ts * 512:
                                                  kc * 1024 + (ts + 1) * 512],
                                         kc == 0, kc == 15)
                            if kcs == 3 and m == 3:
                                s_wqcd.inc(i_)
                if ts == 0:
                    s_zev.wait(t, 1)
            s_zev.wait(t, 2)
            for hu in range(16):
                tt, half = hu // 2, hu % 2
                sl = hu % 2
                if hu >= 2:
                    s_qps.wait(t, hu - 1)
                for m in range(4):
                    lhsT = z_sb[:, m * 1024 + tt * 128: m * 1024 + (tt + 1) * 128]
                    for db in range(2):
                        c0 = m * 2048 + half * 1024 + db * 512
                        i_ = self.mm(t, ps_zq[sl][:, db * 512:(db + 1) * 512], lhsT,
                                     wqup_sb[:, c0:c0 + 512], m == 0, m == 3,
                                     ldw=(db == 0))
                        if m == 3 and db == 1:
                            s_qmm.inc(i_)
                if hu >= 1:
                    self._p2_qtr(t, hu - 1, qn_sb, ps_t2, s_qn, s_qte, s_qtr)
            self._p2_qtr(t, 15, qn_sb, ps_t2, s_qn, s_qte, s_qtr)
            for j in range(12):
                sl = j % 2
                if j >= 2:
                    s_kps.wait(t, j - 1)
                for kc in range(16):
                    if j < 4:
                        x = self.xnA[:, kc * 512 + j * 128: kc * 512 + (j + 1) * 128]
                    else:
                        x = self.xnB[:, kc * 1024 + (j - 4) * 128:
                                     kc * 1024 + (j - 3) * 128]
                    i_ = self.mm(t, ps_kv[sl][:, :], x,
                                 wkv_sb[:, kc * 256:(kc + 1) * 256], kc == 0, kc == 15)
                    if kc == 15:
                        s_kmm.inc(i_)
                if j >= 1:
                    self._p2_ktr(t, j - 1, kn_sb, ps_t2, s_kn, s_kte, s_ktr, s_qte)
            self._p2_ktr(t, 11, kn_sb, ps_t2, s_kn, s_kte, s_ktr, s_qte)
            s_qps.wait(t, 16)
            for c in range(256):
                kcg, kcs = c // 16, c % 16
                s_cpl[kcg % 2].wait(t, 16 * (kcg // 2 + 1))
                lhsT = wcp_sb[kcg % 2][:, kcs * 128:(kcs + 1) * 128]
                rB = self.xnB[:, kcs * 1024:(kcs + 1) * 1024].rearrange(
                    "p (n m) -> p n m", m=16)[:, :, kcg]
                rC = self.xnC[:, kcs * 1024:(kcs + 1) * 1024].rearrange(
                    "p (n m) -> p n m", m=16)[:, :, kcg]
                self.mm(t, ps_zq[0][:, 0:64], lhsT, rB, c == 0, c == 255)
                i2 = self.mm(t, ps_zq[0][:, 512:576], lhsT, rC, c == 0, c == 255,
                             ldw=False)
                if kcs == 15:
                    s_ckmm.inc(i2, 16)
            s_kte.wait(t, 12)
            s_ckv.wait(t, 1)
            i_ = self.mm(t, ps_t2[0][:, 0:128], cr_sb[:, :], self.ident_sb[:, :],
                         True, True)
            s_cktr.inc(i_)
            s_ckv.wait(t, 2)
            i_ = self.mm(t, ps_t2[1][:, 0:128], self.ck_tm[:, :], self.ident_sb[:, :],
                         True, True)
            s_cktr.inc(i_)

        @blk.vector
        def _(v):
            for ts in range(2):
                s_wqcd.wait(v, 4 * (ts + 1))
                for m in range(4):
                    i_ = v.tensor_copy(z_sb[:, m * 1024 + ts * 512:
                                            m * 1024 + (ts + 1) * 512],
                                       ps_zq[m // 2][:, (m % 2) * 512:(m % 2 + 1) * 512])
                s_zev.inc(i_)
            for hu in range(16):
                tt, half = hu // 2, hu % 2
                sl = hu % 2
                psq = ps_zq[sl]
                o = (4 + tt) * 32
                s_qmm.wait(v, hu + 1)
                cosb = self.cos_sb[:, o:o + 32].unsqueeze(1).broadcast_to([128, 8, 32])
                sinb = self.sin_sb[:, o:o + 32].unsqueeze(1).broadcast_to([128, 8, 32])
                p3d = psq[:, :].rearrange("p (h j) -> p h j", j=128)
                q3d = qrot[sl][:, :].rearrange("p (h j) -> p h j", j=128)
                ta = qtmp[0][:, :].rearrange("p (h j) -> p h j", j=32)
                tb = qtmp[1][:, :].rearrange("p (h j) -> p h j", j=32)
                v.tensor_tensor(ta, p3d[:, :, 0:32], cosb, ALU.mult)
                v.tensor_tensor(tb, p3d[:, :, 32:64], sinb, ALU.mult)
                v.drain()
                v.tensor_tensor(q3d[:, :, 0:32], ta, tb, ALU.subtract)
                v.drain()
                v.tensor_tensor(ta, p3d[:, :, 0:32], sinb, ALU.mult)
                v.tensor_tensor(tb, p3d[:, :, 32:64], cosb, ALU.mult)
                v.drain()
                v.tensor_tensor(q3d[:, :, 32:64], ta, tb, ALU.add)
                s_qps.inc(v.tensor_copy(q3d[:, :, 64:128], p3d[:, :, 64:128]))
                v.drain()
                v.tensor_mul(qsq[:, :], qrot[sl][:, :], qrot[sl][:, :])
                v.drain()
                ss = qst[:, sl * 32 + 0: sl * 32 + 8]
                srt = qst[:, sl * 32 + 8: sl * 32 + 16]
                r0 = qst[:, sl * 32 + 16: sl * 32 + 24]
                msq = qst[:, sl * 32 + 24: sl * 32 + 32]
                s_qrs.inc(v.reduce_sum(
                    ss, qsq[:, :].rearrange("p (h j) -> p h j", j=128), axis=AX.X))
                s_qsr.wait(v, hu + 1)
                v_rsqrt(v, ss, srt, r0, msq)
                for hh in range(8):
                    v.tensor_scalar(qrot[sl][:, hh * 128:(hh + 1) * 128],
                                    qrot[sl][:, hh * 128:(hh + 1) * 128],
                                    r0[:, hh:hh + 1], None, ALU.mult)
                v.drain()
                qnw_b = self.qnwb_sb[:, :].unsqueeze(1).broadcast_to([128, 8, 128])
                if hu >= 2:
                    s_qtr.wait(v, 2 * (hu - 1))
                s_qn.inc(v.tensor_tensor(
                    qn_sb[sl][:, :].rearrange("p (h j) -> p h j", j=128),
                    q3d, qnw_b, ALU.mult))
                if hu >= 1:
                    self._p2_qte(v, hu - 1, ps_t2, s_qtr, s_qte)
            self._p2_qte(v, 15, ps_t2, s_qtr, s_qte)
            for j in range(12):
                sl = j % 2
                pk = ps_kv[sl][:, :]
                s_kmm.wait(v, j + 1)
                v.tensor_copy(self.v_tm[:, j * 128:(j + 1) * 128], pk[:, 128:256])
                cosk = self.cos_sb[:, j * 32:(j + 1) * 32]
                sik = self.sin_sb[:, j * 32:(j + 1) * 32]
                v.tensor_tensor(ktmp[:, 0:32], pk[:, 0:32], cosk, ALU.mult)
                v.tensor_tensor(ktmp[:, 32:64], pk[:, 32:64], sik, ALU.mult)
                v.drain()
                v.tensor_tensor(krot[:, 0:32], ktmp[:, 0:32], ktmp[:, 32:64],
                                ALU.subtract)
                v.drain()
                v.tensor_tensor(ktmp[:, 0:32], pk[:, 0:32], sik, ALU.mult)
                v.tensor_tensor(ktmp[:, 32:64], pk[:, 32:64], cosk, ALU.mult)
                v.drain()
                v.tensor_tensor(krot[:, 32:64], ktmp[:, 0:32], ktmp[:, 32:64], ALU.add)
                s_kps.inc(v.tensor_copy(krot[:, 64:128], pk[:, 64:128]))
                v.drain()
                v.tensor_mul(ksq[:, :], krot[:, :], krot[:, :])
                v.drain()
                s_krs.inc(v.reduce_sum(kst[:, 0:1], ksq[:, :], axis=AX.X))
                s_ksr.wait(v, j + 1)
                v_rsqrt(v, kst[:, 0:1], kst[:, 1:2], kst[:, 2:3], kst[:, 3:4])
                v.tensor_scalar(krot[:, :], krot[:, :], kst[:, 2:3], None, ALU.mult)
                v.drain()
                if j >= 1:
                    s_ktr.wait(v, j)
                s_kn.inc(v.tensor_tensor(kn_sb[:, :], krot[:, :], self.knwb_sb[:, :],
                                         ALU.mult))
                if j >= 1:
                    self._p2_kte(v, j - 1, ps_t2, s_ktr, s_kte)
            self._p2_kte(v, 11, ps_t2, s_ktr, s_kte)
            s_ckmm.wait(v, 256)
            s_ckv.inc(v.tensor_copy(
                cr_sb[:, :].rearrange("p (b j) -> p b j", j=64),
                ps_zq[0][:, :].rearrange("p (b j) -> p b j", j=512)[:, :, 0:64]))
            s_cktr.wait(v, 1)
            v.tensor_copy(ctf[:, :], ps_t2[0][:, 0:128])
            v.drain()
            v.tensor_mul(csq[:, :], ctf[:, :], ctf[:, :])
            v.drain()
            s_krs.inc(v.reduce_sum(kst[:, 0:1], csq[:, :], axis=AX.X))
            s_ksr.wait(v, 13)
            v_rsqrt(v, kst[:, 0:1], kst[:, 1:2], kst[:, 2:3], kst[:, 3:4])
            v.tensor_scalar(ctf[:, :], ctf[:, :], kst[:, 2:3], None, ALU.mult)
            v.drain()
            s_ckv.inc(v.tensor_tensor(self.ck_tm[:, :], ctf[:, :], self.knwb_sb[:, :],
                                      ALU.mult))
            s_cktr.wait(v, 2)
            s_ckv.inc(v.tensor_copy(self.ck_fm[:, :], ps_t2[1][:, 0:128]))

        @blk.scalar
        def _(s):
            for hu in range(16):
                sl = hu % 2
                s_qrs.wait(s, hu + 1)
                if hu >= 2:
                    s.drain()
                s_qsr.inc(s.activation(qst[:, sl * 32 + 8: sl * 32 + 16],
                                       qst[:, sl * 32 + 0: sl * 32 + 8], AF.Sqrt,
                                       scale=1.0 / 128.0, bias=self.const_sb[:, 1:2]))
            for j in range(13):
                s_krs.wait(s, j + 1)
                s.drain()
                s_ksr.inc(s.activation(kst[:, 1:2], kst[:, 0:1], AF.Sqrt,
                                       scale=1.0 / 128.0, bias=self.const_sb[:, 1:2]))

        self.s_p2_done = s_ckv
        self.dbg_dump3("p2", [("q_fm", self.q_fm), ("k_fm", self.k_fm),
                              ("v_tm", self.v_tm), ("ck_tm", self.ck_tm),
                              ("ck_fm", self.ck_fm)], s_ckv, BF16)
        ph.close()

    def _p2_qtr(self, t, hu, qn_sb, ps_t2, s_qn, s_qte, s_qtr):
        sl = hu % 2
        s_qn.wait(t, hu + 1)
        for grp in range(2):
            gi = hu * 2 + grp
            if gi >= 2:
                s_qte.wait(t, gi - 1)
            last = None
            for j in range(4):
                hh = grp * 4 + j
                last = self.mm(t, ps_t2[gi % 2][:, j * 128:(j + 1) * 128],
                               qn_sb[sl][:, hh * 128:(hh + 1) * 128],
                               self.ident_sb[:, :], True, True)
            s_qtr.inc(last)

    def _p2_qte(self, v, hu, ps_t2, s_qtr, s_qte):
        tt, half = hu // 2, hu % 2
        for grp in range(2):
            gi = hu * 2 + grp
            s_qtr.wait(v, gi + 1)
            h0 = half * 8 + grp * 4
            dst = self.q_fm[:, :].rearrange("p (h tt) -> p h tt", tt=1024)[
                :, h0:h0 + 4, tt * 128:(tt + 1) * 128]
            src = ps_t2[gi % 2][:, :].rearrange("p (j c) -> p j c", c=128)
            s_qte.inc(v.tensor_copy(dst, src))

    def _p2_ktr(self, t, j, kn_sb, ps_t2, s_kn, s_kte, s_ktr, s_qte):
        s_kn.wait(t, j + 1)
        if j < 2:
            s_qte.wait(t, 32)
        else:
            s_kte.wait(t, j - 1)
        s_ktr.inc(self.mm(t, ps_t2[j % 2][:, 0:128], kn_sb[:, :],
                          self.ident_sb[:, :], True, True))

    def _p2_kte(self, v, j, ps_t2, s_ktr, s_kte):
        s_ktr.wait(v, j + 1)
        s_kte.inc(v.tensor_copy(self.k_fm[:, j * 128:(j + 1) * 128],
                                ps_t2[j % 2][:, 0:128]))


    def p3(self):
        self.phase()
        blk, nc = self.blk, self.nc
        sem = self.sem
        ph = ExitStack()
        psb = lambda n, s, dt=F32: self.rsb(ph, n, s, dt)
        pps = lambda n, s: ph.enter_context(nc.psum_tensor(n, list(s), F32))

        self.ao = self.at("ao", (128, 16 * 1024), BF16, 120)

        tnh = [psb(f"tnh{i}", (128, 640)) for i in range(2)]
        praw = [psb(f"praw{i}", (128, 772), BF16) for i in range(2)]
        pn = [psb(f"pn{i}", (128, 772), BF16) for i in range(2)]
        pT = self.at("pT", (128, 16 * 768), BF16, 48)
        att = psb("att", (128, 16), F32)

        self.stTR.close()
        ps_sc = [pps(f"ps_sc{i}", (128, 1024)) for i in range(2)]
        ps_tr = [pps(f"ps_tr{i}", (128, 768)) for i in range(2)]
        assert nc.psum_banks_remaining == 0, nc.psum_banks_remaining

        s_scm = sem("p3scm")
        s_vnm = sem("p3vnm")
        s_exp = sem("p3exp")
        s_pn = sem("p3pn")
        s_ptr = sem("p3ptr")
        s_pte = sem("p3pte")
        s_omm = sem("p3omm")
        s_oev = sem("p3oev")

        gate = self.s_p2_done
        GATE = gate.n
        bankmap = [ps_tr[0][:, 0:128], ps_tr[0][:, 512:640],
                   ps_tr[1][:, 0:128], ps_tr[1][:, 512:640],
                   ps_sc[0][:, 0:128], ps_sc[0][:, 512:640],
                   ps_sc[1][:, 0:128], ps_sc[1][:, 512:640]]

        @blk.tensor
        def _(t):
            gate.wait(t, GATE)
            for qt in range(8):
                for h in range(16):
                    u = qt * 16 + h
                    sl = u % 2
                    if u >= 2:
                        s_exp.wait(t, u - 1)
                        s_vnm.wait(t, u - 1)
                    if h <= 1:
                        s_oev.wait(t, 2 * qt)
                    qh = self.q_fm[:, h * 1024 + qt * 128: h * 1024 + (qt + 1) * 128]
                    self.mm(t, ps_sc[sl][:, 0:512], qh,
                            self.k_fm[:, qt * 128: qt * 128 + 512], True, True)
                    self.mm(t, ps_sc[sl][:, 512:640], qh,
                            self.k_fm[:, qt * 128 + 512: qt * 128 + 640], True, True,
                            ldw=False)
                    i_ = self.mm(t, ps_sc[sl][:, 640:768], qh, self.ck_fm[:, :],
                                 True, True, ldw=False)
                    s_scm.inc(i_)
                    if h >= 2:
                        self._p3_tr(t, u - 2, pn, ps_tr, s_pn, s_ptr, s_pte, s_oev)
                self._p3_tr(t, qt * 16 + 14, pn, ps_tr, s_pn, s_ptr, s_pte, s_oev)
                self._p3_tr(t, qt * 16 + 15, pn, ps_tr, s_pn, s_ptr, s_pte, s_oev)
                s_pte.wait(t, 16 * (qt + 1))
                for h8 in range(2):
                    if h8 == 1:
                        s_oev.wait(t, 2 * qt + 1)
                    for b in range(6):
                        if b < 5:
                            lhsT = self.v_tm[:, (qt + b) * 128:(qt + b + 1) * 128]
                        else:
                            lhsT = self.ck_tm[:, :]
                        for j in range(8):
                            h = h8 * 8 + j
                            i_ = self.mm(t, bankmap[j], lhsT,
                                         pT[:, h * 768 + b * 128: h * 768 + (b + 1) * 128],
                                         b == 0, b == 5, ldw=(j == 0))
                            if b == 5 and j == 7:
                                s_omm.inc(i_)

        def pT_evac(v, u):
            sl = u % 2
            h_ = u % 16
            s_ptr.wait(v, u + 1)
            s_pte.inc(v.tensor_copy(pT[:, h_ * 768:(h_ + 1) * 768],
                                    ps_tr[sl][:, 0:768]))

        def v_tail(v, u):
            sl = u % 2
            qt_ = u // 16
            ds = att[:, sl * 8 + 0: sl * 8 + 1]
            dc = att[:, sl * 8 + 1: sl * 8 + 2]
            s_exp.wait(v, u + 1)
            v.scalar_tensor_tensor(
                pn[sl][:, 0:641], praw[sl][:, 0:641], 1.0,
                self.msw_sb[:, qt_ * 641:(qt_ + 1) * 641], ALU.mult, ALU.mult,
                accum_out=ds)
            v.scalar_tensor_tensor(
                pn[sl][:, 641:770], praw[sl][:, 641:770], 1.0,
                self.msc_sb[:, qt_ * 129:(qt_ + 1) * 129], ALU.mult, ALU.mult,
                accum_out=dc)
            v.drain()
            v.reciprocal(att[:, sl * 8: sl * 8 + 2], att[:, sl * 8: sl * 8 + 2])
            v.drain()
            v.tensor_scalar(pn[sl][:, 0:641], pn[sl][:, 0:641], ds, None,
                            ALU.mult)
            s_pn.inc(v.tensor_scalar(pn[sl][:, 641:769], pn[sl][:, 641:769],
                                     dc, None, ALU.mult))

        @blk.vector
        def _(v):
            for qt in range(8):
                for h in range(16):
                    u = qt * 16 + h
                    sl = u % 2
                    rm = att[:, sl * 8 + 2: sl * 8 + 3]
                    nm = att[:, sl * 8 + 3: sl * 8 + 4]
                    s_scm.wait(v, u + 1)
                    v.reduce_max(rm, ps_sc[sl][:, 640:768], axis=AX.X)
                    v.drain()
                    s_vnm.inc(v.tensor_scalar(nm, rm, -SC0, self.negsink_sb[:, h:h + 1],
                                              ALU.mult, ALU.min))
                    if u >= 1 and (u - 1) % 16 <= 14:
                        v_tail(v, u - 1)
                    if u >= 2 and (u - 2) % 16 <= 13:
                        pT_evac(v, u - 2)
                v_tail(v, qt * 16 + 15)
                pT_evac(v, qt * 16 + 14)
                pT_evac(v, qt * 16 + 15)
                for h8 in range(2):
                    s_omm.wait(v, 2 * qt + h8 + 1)
                    last = None
                    for j in range(8):
                        h = h8 * 8 + j
                        last = v.tensor_copy(
                            self.ao[:, h * 1024 + qt * 128: h * 1024 + (qt + 1) * 128],
                            bankmap[j])
                    s_oev.inc(last)

        @blk.scalar
        def _(s):
            for u in range(128):
                h = u % 16
                sl = u % 2
                if u >= 1:
                    s.drain()
                s_scm.wait(s, u + 1)
                s.activation(tnh[sl][:, :], ps_sc[sl][:, 0:640], AF.Tanh,
                             scale=float(SC0 / CAP))
                s.drain()
                if u >= 2:
                    s_pn.wait(s, u - 1)
                s.activation(praw[sl][:, 0:640], tnh[sl][:, :], AF.Exp,
                             scale=float(CAP), bias=self.negm_sb[:, h:h + 1])
                s.activation(praw[sl][:, 640:641], self.sksc_sb[:, h:h + 1],
                             AF.Exp, scale=float(SC0), bias=self.negm_sb[:, h:h + 1])
                s_vnm.wait(s, u + 1)
                nm = att[:, sl * 8 + 3: sl * 8 + 4]
                s.activation(praw[sl][:, 641:769], ps_sc[sl][:, 640:768],
                             AF.Exp, scale=float(SC0), bias=nm)
                s_exp.inc(s.activation(praw[sl][:, 769:770], self.sksc_sb[:, h:h + 1],
                                       AF.Exp, scale=float(SC0), bias=nm))

        self.s_p3_done = s_oev
        self.s_p3_omm = s_omm
        self.s_p3_pte = s_pte
        self.s_p3_scm = s_scm
        self.dbg_dump3("p3", [("ao", self.ao)], s_oev, BF16)
        ph.close()

    def _p3_tr(self, t, u, pn, ps_tr, s_pn, s_ptr, s_pte, s_oev):
        sl = u % 2
        s_pn.wait(t, u + 1)
        if u % 16 >= 2:
            s_pte.wait(t, u - 1)
        else:
            s_oev.wait(t, 2 * (u // 16))
        last = None
        for b in range(6):
            src = pn[sl][:, b * 128:(b + 1) * 128] if b < 5 \
                else pn[sl][:, 641:769]
            last = self.mm(t, ps_tr[sl][:, b * 128:(b + 1) * 128],
                           src, self.ident_sb[:, :], True, True)
        s_ptr.inc(last)


    def p4(self):
        self.phase()
        blk, d, nc = self.blk, self.d, self.nc
        sem = self.sem
        ph = ExitStack()
        pps = lambda n, s: ph.enter_context(nc.psum_tensor(n, list(s), F32))

        wg_buf = [self.at(f"wg_buf{i}", (128, 4 * 2048), BF16, 48 + 16 * i)
                  for i in range(2)]
        ibuf = [self.at(f"ibuf{i}", (128, 4 * 1024), BF16, 80 + 8 * i)
                for i in range(2)]
        ps_int = [pps(f"ps_int{i}", (128, 1024)) for i in range(2)]

        s_wg = [sem(f"p4wg{i}", dma=True) for i in range(2)]
        s_imm = sem("p4imm")
        s_iev = sem("p4iev")
        s_isp = [sem("p4isp0", dma=True), sem("p4isp1", dma=True)]
        self.s_isp = s_isp

        g3 = [(self.s_p3_done, self.s_p3_done.n), (self.s_p3_omm, self.s_p3_omm.n),
              (self.s_p3_pte, self.s_p3_pte.n), (self.s_p3_scm, self.s_p3_scm.n)]

        @blk.gpsimd
        def _(g):
            for gs, gn in g3:
                gs.wait(g, gn)
            for gg in range(4):
                if gg >= 2:
                    s_imm.wait(g, 16 * (gg - 1))
                for fc in range(4):
                    s_wg[gg % 2].inc(g.dma_start(
                        out=wg_buf[gg % 2][:, fc * 2048:(fc + 1) * 2048],
                        in_=d["wg"][gg, fc]), 16)

        @blk.tensor
        def _(t):
            self.s_p3_done.wait(t, self.s_p3_done.n)
            for gg in range(4):
                s_wg[gg % 2].wait(t, 64 * (gg // 2 + 1))
                for icl in range(16):
                    ic = gg * 16 + icl
                    sl = ic % 2
                    if ic >= 2:
                        s_iev.wait(t, ic - 1)
                    for fc in range(4):
                        lhsT = wg_buf[gg % 2][:, fc * 2048 + icl * 128:
                                              fc * 2048 + (icl + 1) * 128]
                        for ts in range(2):
                            i_ = self.mm(t, ps_int[sl][:, ts * 512:(ts + 1) * 512],
                                         lhsT,
                                         self.ao[:, (gg * 4 + fc) * 1024 + ts * 512:
                                                 (gg * 4 + fc) * 1024 + (ts + 1) * 512],
                                         fc == 0, fc == 3, ldw=(ts == 0))
                            if fc == 3 and ts == 1:
                                s_imm.inc(i_)

        @blk.vector
        def _(v):
            for ic in range(64):
                sl = ic % 2
                b = ic // 4
                s_imm.wait(v, ic + 1)
                if b >= 2 and ic % 4 == 0:
                    s_isp[b % 2].wait(v, 16 * (b // 2))
                s_iev.inc(v.tensor_copy(
                    ibuf[b % 2][:, (ic % 4) * 1024:(ic % 4 + 1) * 1024],
                    ps_int[sl][:, :]))

        @blk.sync
        def _(sy):
            for b in range(16):
                s_iev.wait(sy, 4 * (b + 1))
                s_isp[b % 2].inc(sy.dma_start(
                    out=self.inter_s[4 * b:4 * (b + 1)].transpose([1, 0, 2]),
                    in_=ibuf[b % 2][:, :].rearrange("p (b c) -> p b c", c=1024)), 16)

        self.s_p4_done = s_iev
        ph.close()

    def p5(self):
        self.phase()
        blk, d, nc = self.blk, self.d, self.nc
        sem = self.sem
        ph = ExitStack()
        pps = lambda n, s: ph.enter_context(nc.psum_tensor(n, list(s), F32))

        self.s0 = self.at("s0", (128, 16 * 1024), BF16, 32)
        wo_buf = [self.at(f"wo_buf{i}", (128, 4 * 512), BF16, 72 + 4 * i)
                  for i in range(2)]
        int_buf = [self.at(f"int_buf{i}", (128, 4 * 1024), BF16, 80 + 8 * i)
                   for i in range(2)]
        ps_o = [pps(f"ps_o{i}", (128, 512)) for i in range(8)]

        s_wol = [sem(f"p5wol{i}", dma=True) for i in range(2)]
        s_inl = [sem(f"p5inl{i}", dma=True) for i in range(2)]
        s_wof = sem("p5wof")
        s_wev = sem("p5wev")

        @blk.gpsimd
        def _(g):
            self.s_p4_done.wait(g, self.s_p4_done.n)
            for u in range(64):
                s, icg = u // 16, u % 16
                if u >= 2:
                    s_wof.wait(g, u - 1)
                s_wol[u % 2].inc(g.dma_start(
                    out=wo_buf[u % 2][:, :],
                    in_=d["wo0"][s, icg].rearrange("p k c -> p (k c)")), 16)

        @blk.sync
        def _(sy):
            for u in range(64):
                s, icg = u // 16, u % 16
                if u < 2:
                    self.s_isp[0].wait(sy, self.s_isp[0].n)
                    self.s_isp[1].wait(sy, self.s_isp[1].n)
                else:
                    s_wof.wait(sy, u - 1)
                s_inl[u % 2].inc(sy.dma_start(
                    out=int_buf[u % 2][:, :].rearrange("p (b c) -> p b c", c=1024),
                    in_=self.inter_s[4 * icg:4 * (icg + 1)].transpose([1, 0, 2])), 16)

        @blk.tensor
        def _(t):
            self.s_p4_done.wait(t, self.s_p4_done.n)
            for s in range(4):
                if s >= 1:
                    s_wev.wait(t, s)
                for icg in range(16):
                    u = s * 16 + icg
                    s_wol[u % 2].wait(t, 16 * (u // 2 + 1))
                    s_inl[u % 2].wait(t, 16 * (u // 2 + 1))
                    for kin in range(4):
                        ic = icg * 4 + kin
                        for ocl in range(4):
                            lhsT = wo_buf[u % 2][:, kin * 512 + ocl * 128:
                                                 kin * 512 + (ocl + 1) * 128]
                            for ts in range(2):
                                i_ = self.mm(t, ps_o[ocl * 2 + ts][:, :], lhsT,
                                             int_buf[u % 2][:, kin * 1024 + ts * 512:
                                                            kin * 1024 + (ts + 1) * 512],
                                             ic == 0, ic == 63, ldw=(ts == 0))
                                if kin == 3 and ocl == 3 and ts == 1:
                                    s_wof.inc(i_)

        @blk.vector
        def _(v):
            for s in range(4):
                s_wof.wait(v, 16 * (s + 1))
                last = None
                for ocl in range(4):
                    oc = s * 4 + ocl
                    for ts in range(2):
                        last = v.tensor_copy(
                            self.s0[:, oc * 1024 + ts * 512: oc * 1024 + (ts + 1) * 512],
                            ps_o[ocl * 2 + ts][:, :])
                s_wev.inc(last)

        self.s_p5_done = s_wev
        self.dbg_dump3("p5", [("s0", self.s0)], s_wev, BF16)
        ph.close()


    def p6(self):
        self.phase()
        blk, d, nc = self.blk, self.d, self.nc
        sem = self.sem
        ph = ExitStack()
        psb = lambda n, s, dt=F32: self.rsb(ph, n, s, dt)

        hdn = self.at("hdn0", (128, NI0 * 1024), BF16, 64)
        gbuf = [self.at(f"p6g{i}", (128, 1024), BF16, 152 + 2 * i) for i in range(2)]
        ubuf = [self.at(f"p6u{i}", (128, 1024), BF16, 156 + 2 * i) for i in range(2)]
        t1 = psb("p6t1", (128, 2048))
        wdbuf = [self.at(f"p6wd{i}", (128, 4 * 512), BF16, 0 + 4 * i) for i in range(2)]
        t2 = [self.at(f"p6t2{i}", (128, 512), F32, 8 + 2 * i) for i in range(2)]
        x0b = [self.at(f"p6x0{i}", (128, 1024), F32, 12 + 4 * i) for i in range(2)]
        x1b = [self.at(f"p6x1{i}", (128, 1024), F32, 20 + 4 * i) for i in range(2)]
        m1 = self.at("p6m1", (128, 512), F32, 28)

        s_gl = [sem(f"p6gl{i}", dma=True) for i in range(2)]
        s_ul = [sem(f"p6ul{i}", dma=True) for i in range(2)]
        s_gtf = sem("p6gtf")
        s_utf = sem("p6utf")
        s_sil = sem("p6sil")
        s_hdn = sem("p6hdn")
        s_wdl = [sem(f"p6wdl{i}", dma=True) for i in range(2)]
        s_dwf = sem("p6dwf")
        s_dsil = sem("p6dsil")
        s_fus = sem("p6fus")
        s_x0l = [sem(f"p6x0l{i}", dma=True) for i in range(2)]
        s_x1s = [sem(f"p6x1s{i}", dma=True) for i in range(2)]
        self.s_x1s = s_x1s

        gate = self.s_p5_done
        GATE = gate.n

        @blk.gpsimd
        def _(g):
            gate.wait(g, GATE)
            for cb in range(22):
                for kcg in range(4):
                    u = cb * 4 + kcg
                    if u >= 2:
                        s_gtf.wait(g, u - 1)
                    s_gl[u % 2].inc(g.dma_start(
                        out=gbuf[u % 2][:, :],
                        in_=d["wgate0"][cb, kcg].rearrange("p k c -> p (k c)")), 16)
                for kcg in range(4):
                    u = cb * 4 + kcg
                    if u >= 2:
                        s_utf.wait(g, u - 1)
                    s_ul[u % 2].inc(g.dma_start(
                        out=ubuf[u % 2][:, :],
                        in_=d["wup0"][cb, kcg].rearrange("p k c -> p (k c)")), 16)
            for u in range(44):
                sw, icg = u // 11, u % 11
                if u < 2:
                    s_utf.wait(g, 88)
                else:
                    s_dwf.wait(g, u - 1)
                s_wdl[u % 2].inc(g.dma_start(
                    out=wdbuf[u % 2][:, :],
                    in_=d["wdown0"][sw, icg].rearrange("p k c -> p (k c)")), 16)

        phA = ExitStack()
        ps_gu = [[phA.enter_context(nc.psum_tensor(f"ps_gu{a}{i}", [128, 1024], F32))
                  for i in range(2)] for a in range(2)]

        @blk.tensor
        def _(t):
            gate.wait(t, GATE)
            for cb in range(22):
                if cb >= 1:
                    s_sil.wait(t, 4 * cb)
                for kcg in range(4):
                    u = cb * 4 + kcg
                    s_gl[u % 2].wait(t, 16 * (u // 2 + 1))
                    for kin in range(4):
                        for icl in range(2):
                            lhsT = gbuf[u % 2][:, kin * 256 + icl * 128:
                                               kin * 256 + (icl + 1) * 128]
                            for ts in range(2):
                                i_ = self.mm(t, ps_gu[0][icl][:, ts * 512:(ts + 1) * 512],
                                             lhsT,
                                             self.xnB[:, (kcg * 4 + kin) * 1024 + ts * 512:
                                                      (kcg * 4 + kin) * 1024 + (ts + 1) * 512],
                                             kcg + kin == 0, kcg == 3 and kin == 3,
                                             ldw=(ts == 0))
                                if kin == 3 and icl == 1 and ts == 1:
                                    s_gtf.inc(i_)
                if cb >= 1:
                    s_hdn.wait(t, 4 * cb)
                for kcg in range(4):
                    u = cb * 4 + kcg
                    s_ul[u % 2].wait(t, 16 * (u // 2 + 1))
                    for kin in range(4):
                        for icl in range(2):
                            lhsT = ubuf[u % 2][:, kin * 256 + icl * 128:
                                               kin * 256 + (icl + 1) * 128]
                            for ts in range(2):
                                i_ = self.mm(t, ps_gu[1][icl][:, ts * 512:(ts + 1) * 512],
                                             lhsT,
                                             self.xnB[:, (kcg * 4 + kin) * 1024 + ts * 512:
                                                      (kcg * 4 + kin) * 1024 + (ts + 1) * 512],
                                             kcg + kin == 0, kcg == 3 and kin == 3,
                                             ldw=(ts == 0))
                                if kin == 3 and icl == 1 and ts == 1:
                                    s_utf.inc(i_)

        @blk.scalar
        def _(s):
            for cb in range(22):
                s_gtf.wait(s, 4 * (cb + 1))
                if cb >= 1:
                    s_hdn.wait(s, 4 * cb)
                for icl in range(2):
                    ic = cb * 2 + icl
                    for ts in range(2):
                        if cb + icl + ts:
                            s.drain()
                        s_sil.inc(s.activation(
                            t1[:, (icl * 2 + ts) * 512:(icl * 2 + ts + 1) * 512],
                            ps_gu[0][icl][:, ts * 512:(ts + 1) * 512], AF.Silu,
                            scale=self.gs0_sb[:, ic:ic + 1]))

        @blk.vector
        def _(v):
            for cb in range(22):
                s_utf.wait(v, 4 * (cb + 1))
                s_sil.wait(v, 4 * (cb + 1))
                for icl in range(2):
                    ic = cb * 2 + icl
                    for ts in range(2):
                        s_hdn.inc(v.scalar_tensor_tensor(
                            hdn[:, ic * 1024 + ts * 512: ic * 1024 + (ts + 1) * 512],
                            ps_gu[1][icl][:, ts * 512:(ts + 1) * 512],
                            self.us0_sb[:, ic:ic + 1],
                            t1[:, (icl * 2 + ts) * 512:(icl * 2 + ts + 1) * 512],
                            ALU.mult, ALU.mult))

        phA.close()
        ps_d = [ph.enter_context(nc.psum_tensor(f"ps_d{i}", [128, 512], F32))
                for i in range(8)]

        @blk.tensor
        def _(t):
            s_hdn.wait(t, 88)
            s_sil.wait(t, 88)
            for sw in range(4):
                if sw >= 1:
                    s_dsil.wait(t, 8 * sw)
                for icg in range(11):
                    u = sw * 11 + icg
                    s_wdl[u % 2].wait(t, 16 * (u // 2 + 1))
                    for kin in range(4):
                        ic = icg * 4 + kin
                        for ocl in range(4):
                            lhsT = wdbuf[u % 2][:, kin * 512 + ocl * 128:
                                                kin * 512 + (ocl + 1) * 128]
                            for ts in range(2):
                                i_ = self.mm(t, ps_d[ocl * 2 + ts][:, :], lhsT,
                                             hdn[:, ic * 1024 + ts * 512:
                                                 ic * 1024 + (ts + 1) * 512],
                                             ic == 0, ic == 43, ldw=(ts == 0))
                                if kin == 3 and ocl == 3 and ts == 1:
                                    s_dwf.inc(i_)

        @blk.scalar
        def _(s):
            for u in range(32):
                sw = u // 8
                ocl, ts = (u % 8) // 2, u % 2
                oc = sw * 4 + ocl
                sl = u % 2
                s_dwf.wait(s, 11 * (sw + 1))
                if u >= 2:
                    s_fus.wait(s, u - 1)
                s.drain()
                s_dsil.inc(s.activation(t2[sl][:, :], ps_d[ocl * 2 + ts][:, :],
                                        AF.Silu, scale=self.ds0_sb[:, oc:oc + 1]))

        @blk.vector
        def _(v):
            for u in range(32):
                sw = u // 8
                ocl, ts = (u % 8) // 2, u % 2
                oc = sw * 4 + ocl
                sl = u % 2
                s_dsil.wait(v, u + 1)
                s_x0l[oc % 2].wait(v, 16 * (oc // 2 + 1))
                v.drain()
                v.tensor_tensor(m1[:, :], t2[sl][:, :],
                                self.s0[:, oc * 1024 + ts * 512:
                                        oc * 1024 + (ts + 1) * 512], ALU.mult)
                v.drain()
                if u >= 4:
                    s_x1s[oc % 2].wait(v, 16 * (oc // 2))
                s_fus.inc(v.tensor_add(
                    x1b[oc % 2][:, ts * 512:(ts + 1) * 512], m1[:, :],
                    x0b[oc % 2][:, ts * 512:(ts + 1) * 512]))

        @blk.sync
        def _(sy):
            for oc in range(16):
                if oc < 2:
                    s_utf.wait(sy, 88)
                else:
                    s_fus.wait(sy, 2 * (oc - 1))
                s_x0l[oc % 2].inc(sy.dma_start(out=x0b[oc % 2][:, :],
                                               in_=d["x0f"][oc]), 16)
                if oc >= 1:
                    s_fus.wait(sy, 2 * oc)
                    s_x1s[(oc - 1) % 2].inc(sy.dma_start(
                        out=self.x1_s[oc - 1], in_=x1b[(oc - 1) % 2][:, :]), 16)
            s_fus.wait(sy, 32)
            s_x1s[1].inc(sy.dma_start(out=self.x1_s[15], in_=x1b[1][:, :]), 16)

        if "p6" in self.dbg:
            o = nc.dram_tensor("dbg_x1", [NC, 128, T], F32, kind="ExternalOutput").ap()
            s_d6 = self.sem("dbg6", keep=True)

            @blk.sync
            def _(sy):
                s_x1s[0].wait(sy, s_x1s[0].n)
                s_x1s[1].wait(sy, s_x1s[1].n)
                s_d6.inc(sy.dma_start(out=o, in_=self.x1_s[:]), 16)
            self.final_sems.append(s_d6)

        self.s_p6_done = s_fus
        self.s_p6_x1s = s_x1s
        self.s_p6_dwf = s_dwf
        self.s_p6_dsil = s_dsil
        ph.close()


    def p7(self):
        self.phase()
        blk, d, nc = self.blk, self.d, self.nc
        sem = self.sem
        ph = ExitStack()
        psb = lambda n, s, dt=F32: self.rsb(ph, n, s, dt)

        self.xn1 = self.at("xn1", (128, NC * 1024), BF16, 0)
        x1t = [self.at(f"p7x1t{i}", (128, 1024), F32, 32 + 4 * i) for i in range(2)]
        xb = [self.at(f"p7xb{i}", (128, 1024), BF16, 40 + 2 * i) for i in range(2)]
        xq = [self.at(f"p7xq{i}", (128, 1024), BF16, 44 + 2 * i) for i in range(2)]
        tsc = self.at("p7tsc", (128, 512), F32, 48)
        stt = psb("p7stt", (1, 8 * 512))

        ps_s = [ph.enter_context(nc.psum_tensor(f"ps_s{i}", [1, 512], F32))
                for i in range(4)]
        ps_b = [ph.enter_context(nc.psum_tensor(f"ps_b{i}", [128, 512], F32))
                for i in range(4)]

        s_l1 = [sem(f"p7l{i}", dma=True) for i in range(2)]
        s_cv = sem("p7cv")
        s_smm = sem("p7smm")
        s_post = sem("p7post")
        s_sq = sem("p7sq")
        s_bc = sem("p7bc")
        s_l2 = [sem(f"p7l2{i}", dma=True) for i in range(2)]
        s_ap = sem("p7ap")

        gx = self.s_p6_x1s

        @blk.sync
        def _(sy):
            for oc in range(16):
                gx[oc % 2].wait(sy, 16 * (oc // 2 + 1))
                if oc >= 2:
                    s_cv.wait(sy, 2 * (oc - 1))
                s_l1[oc % 2].inc(sy.dma_start(out=x1t[oc % 2][:, :],
                                              in_=self.x1_s[oc]), 16)
            for oc in range(16):
                if oc < 2:
                    s_cv.wait(sy, 32)
                else:
                    s_ap.wait(sy, 2 * (oc - 1))
                s_l2[oc % 2].inc(sy.dma_start(out=x1t[oc % 2][:, :],
                                              in_=self.x1_s[oc]), 16)

        @blk.vector
        def _(v):
            v.drain()
            for oc in range(16):
                s_l1[oc % 2].wait(v, 16 * (oc // 2 + 1))
                if oc >= 2:
                    s_smm.wait(v, 4 * (oc - 1))
                v.tensor_copy(xb[oc % 2][:, :], x1t[oc % 2][:, :])
                v.drain()
                s_cv.inc(v.tensor_mul(xq[oc % 2][:, :], xb[oc % 2][:, :],
                                      xb[oc % 2][:, :]), 2)
            s_smm.wait(v, 64)
            for j in range(4):
                v.tensor_copy(stt[:, j * 512:(j + 1) * 512], ps_s[j][:, :])
            v.drain()
            v.tensor_scalar(stt[:, 0:1024], stt[:, 0:1024], 1.0 / H, None, ALU.mult)
            v.tensor_scalar(stt[:, 1024:2048], stt[:, 1024:2048], 1.0 / H, None, ALU.mult)
            v.drain()
            v.tensor_mul(stt[:, 2048:3072], stt[:, 0:1024], stt[:, 0:1024])
            v.drain()
            s_post.inc(v.tensor_sub(stt[:, 1024:2048], stt[:, 1024:2048],
                                    stt[:, 2048:3072]))
            s_sq.wait(v, 1)
            v.reciprocal(stt[:, 2048:3072], stt[:, 3072:4096])
            v.drain()
            v.tensor_scalar(stt[:, 1024:2048], stt[:, 1024:2048], 1.0,
                            self.const_sb[0:1, 0:1], ALU.mult, ALU.add)
            v.drain()
            v.tensor_mul(stt[:, 3072:4096], stt[:, 2048:3072], stt[:, 2048:3072])
            v.drain()
            v.tensor_mul(stt[:, 3072:4096], stt[:, 3072:4096], stt[:, 1024:2048])
            v.drain()
            v.tensor_scalar(stt[:, 3072:4096], stt[:, 3072:4096], -0.5, 1.5,
                            ALU.mult, ALU.add)
            v.drain()
            v.tensor_mul(stt[:, 2048:3072], stt[:, 2048:3072], stt[:, 3072:4096])
            v.drain()
            s_post.inc(v.tensor_mul(stt[:, 0:1024], stt[:, 0:1024], stt[:, 2048:3072]))
            for oc in range(16):
                s_l2[oc % 2].wait(v, 16 * (oc // 2 + 1))
                s_bc.wait(v, 4)
                for ts in range(2):
                    v.tensor_tensor(tsc[:, :], x1t[oc % 2][:, ts * 512:(ts + 1) * 512],
                                    ps_b[ts][:, :], ALU.mult)
                    v.drain()
                    s_ap.inc(v.tensor_sub(
                        self.xn1[:, oc * 1024 + ts * 512: oc * 1024 + (ts + 1) * 512],
                        tsc[:, :], ps_b[2 + ts][:, :]))
                    v.drain()

        @blk.scalar
        def _(s):
            s.drain()
            s_post.wait(s, 1)
            s_sq.inc(s.activation(stt[:, 3072:4096], stt[:, 1024:2048], AF.Sqrt,
                                  bias=self.const_sb[0:1, 0:1]))

        @blk.tensor
        def _(t):
            self.s_p6_dsil.wait(t, 32)
            for oc in range(16):
                s_cv.wait(t, 2 * (oc + 1))
                for ts in range(2):
                    i1 = self.mm(t, ps_s[ts][:, :], self.onesc_sb[:, :],
                                 xb[oc % 2][:, ts * 512:(ts + 1) * 512],
                                 oc == 0, oc == 15, ldw=(oc == 0 and ts == 0))
                    i2 = self.mm(t, ps_s[2 + ts][:, :], self.onesc_sb[:, :],
                                 xq[oc % 2][:, ts * 512:(ts + 1) * 512],
                                 oc == 0, oc == 15, ldw=False)
                    s_smm.inc(i2, 2)
            s_post.wait(t, 2)
            for ts in range(2):
                s_bc.inc(t.matmul(ps_b[ts][:, :], self.onesrF_sb[:, :],
                                  stt[:, 2048 + ts * 512: 2048 + (ts + 1) * 512],
                                  start=True, stop=True))
                s_bc.inc(t.matmul(ps_b[2 + ts][:, :], self.onesrF_sb[:, :],
                                  stt[:, ts * 512:(ts + 1) * 512],
                                  start=True, stop=True))

        self.s_p7_done = s_ap
        self.s_p7_l2 = s_l2
        self.dbg_dump3("p7", [("xn1", self.xn1)], s_ap, BF16)
        ph.close()

    def p8(self):
        self.phase()
        blk, d, nc = self.blk, self.d, self.nc
        sem = self.sem
        ph = ExitStack()
        psb = lambda n, s, dt=F32: self.rsb(ph, n, s, dt)

        vtm = self.at("vtm1", (128, 8 * 2048), BF16, 32)
        wres = self.at("wres1", (128, 16 * 2048), BF16, 64)
        ktm = self.at("ktm1", (128, 8 * 2048), BF16, 128)
        self.st8L = ExitStack()
        kvbf = self.rsb(self.st8L, "kvbf", (128, 16 * 128), BF16)
        ksf = self.rsb(self.st8L, "ksf", (128, 16))
        kvsb = psb("kvsb", (128, 2064))

        s_wl = [sem(f"p8wl{i}", dma=True) for i in range(8)]
        s_pmm = sem("p8pmm")
        s_pevK = sem("p8pevK")
        s_pevV = sem("p8pevV")
        s_kvm = sem("p8kvm")
        s_kve = sem("p8kve")
        s_cc = sem("p8cc")
        s_cin = sem("p8cin", dma=True)
        s_cout = sem("p8cout", dma=True)
        s_cast = sem("p8cast")

        @blk.gpsimd
        def _(g):
            self.s_p6_dwf.wait(g, self.s_p6_dwf.n)
            for mat in range(2):
                nm = "wk1" if mat == 0 else "wv1"
                if mat == 1:
                    s_pmm.wait(g, 8)
                for kc in range(16):
                    s_wl[mat * 4 + kc // 4].inc(g.dma_start(
                        out=wres[:, kc * 2048:(kc + 1) * 2048],
                        in_=d[nm][kc].rearrange("p a c -> p (a c)")), 16)
            s_cin.wait(g, 16)
            s_cc.inc(g.collective_compute(
                "AllReduce", ALU.add,
                replica_groups=[[0, 1], [2, 3], [4, 5], [6, 7]],
                ins=[self.cc_in.ap().opt()],
                outs=[self.cc_out.ap().opt()]))

        phA = ExitStack()
        ps_p = [phA.enter_context(nc.psum_tensor(f"ps_p8{i}", [128, 2048], F32))
                for i in range(2)]

        @blk.tensor
        def _(t):
            self.s_p7_done.wait(t, self.s_p7_done.n)
            for mat in range(2):
                for tt in range(8):
                    u = mat * 8 + tt
                    sl = u % 2
                    if 2 <= u <= 9:
                        s_pevK.wait(t, u - 1)
                    elif u >= 10:
                        s_pevV.wait(t, u - 9)
                    for kc in range(16):
                        if tt == 0 and kc % 4 == 0:
                            s_wl[mat * 4 + kc // 4].wait(t, 64)
                        lhsT = self.xn1[:, kc * 1024 + tt * 128:
                                        kc * 1024 + (tt + 1) * 128]
                        for cb in range(4):
                            i_ = self.mm(t, ps_p[sl][:, cb * 512:(cb + 1) * 512],
                                         lhsT, wres[:, kc * 2048 + cb * 512:
                                                    kc * 2048 + (cb + 1) * 512],
                                         kc == 0, kc == 15, ldw=(cb == 0))
                            if kc == 15 and cb == 3:
                                s_pmm.inc(i_)

        @blk.scalar
        def _(s):
            s.drain()
            for tt in range(8):
                sl = tt % 2
                s_pmm.wait(s, tt + 1)
                last = None
                for cb in range(4):
                    if tt + cb:
                        s.drain()
                    last = s.activation(ktm[:, tt * 2048 + cb * 512:
                                            tt * 2048 + (cb + 1) * 512],
                                        ps_p[sl][:, cb * 512:(cb + 1) * 512], AF.Relu)
                s_pevK.inc(last)

        @blk.vector
        def _(v):
            v.drain()
            for tt in range(8):
                sl = tt % 2
                s_pmm.wait(v, 8 + tt + 1)
                last = None
                for cb in range(4):
                    last = v.tensor_copy(vtm[:, tt * 2048 + cb * 512:
                                             tt * 2048 + (cb + 1) * 512],
                                         ps_p[sl][:, cb * 512:(cb + 1) * 512])
                s_pevV.inc(last)

        phA.close()
        self.stQ = ExitStack()
        self.ps_q1 = [self.stQ.enter_context(nc.psum_tensor(f"ps_q1{i}",
                                                            [128, 1024], F32))
                      for i in range(2)]
        phB = ExitStack()
        ps_kv = [phB.enter_context(nc.psum_tensor(f"ps_kv1{i}", [128, 128], F32))
                 for i in range(2)]
        ps_ks = [phB.enter_context(nc.psum_tensor(f"ps_ks1{i}", [128, 1], F32))
                 for i in range(2)]

        @blk.tensor
        def _(t):
            s_pevK.wait(t, 8)
            s_pevV.wait(t, 8)
            for pair in range(16):
                sl = pair % 2
                if pair >= 2:
                    s_kve.wait(t, pair - 1)
                for t8 in range(8):
                    lhsT = ktm[:, t8 * 2048 + pair * 128: t8 * 2048 + (pair + 1) * 128]
                    self.mm(t, ps_kv[sl][:, :], lhsT,
                            vtm[:, t8 * 2048 + pair * 128: t8 * 2048 + (pair + 1) * 128],
                            t8 == 0, t8 == 7)
                    i_ = self.mm(t, ps_ks[sl][:, :], lhsT, self.onesc_sb[:, :],
                                 t8 == 0, t8 == 7, ldw=False)
                    if t8 == 7:
                        s_kvm.inc(i_)

        @blk.vector
        def _(v):
            for pair in range(16):
                sl = pair % 2
                s_kvm.wait(v, pair + 1)
                v.tensor_copy(kvsb[:, pair * 128:(pair + 1) * 128], ps_kv[sl][:, :])
                s_kve.inc(v.tensor_copy(kvsb[:, 2048 + pair: 2049 + pair],
                                        ps_ks[sl][:, :]))
            s_cout.wait(v, 16)
            v.drain()
            v.tensor_copy(kvbf[:, :], kvsb[:, 0:2048])
            s_cast.inc(v.tensor_copy(ksf[:, :], kvsb[:, 2048:2064]))

        @blk.sync
        def _(sy):
            s_kve.wait(sy, 16)
            s_cin.inc(sy.dma_start(out=self.cc_in.ap(), in_=kvsb[:, :]), 16)
            s_cc.wait(sy, 1)
            s_cout.inc(sy.dma_start(out=kvsb[:, :], in_=self.cc_out.ap()), 16)

        phB.close()
        self.kvbf, self.ksf = kvbf, ksf
        self.s_p8_kve = s_kve
        self.s_p8_cast = s_cast
        self.s_p8_done = s_cast
        self.s_p8_kvm = s_kvm
        self.s_p8_pmm = s_pmm
        self.s_p8_pevK = s_pevK
        self.s_p8_pevV = s_pevV
        ph.close()

    def p9(self):
        self.phase()
        blk, d, nc = self.blk, self.d, self.nc
        sem = self.sem
        ph = ExitStack()
        psb = lambda n, s, dt=F32: self.rsb(ph, n, s, dt)

        q1 = self.at("q1s", (128, 16 * 1024), BF16, 64)
        self.q1_ref = q1
        wqb = [self.at(f"p9wq{i}", (128, 1024), BF16, 96 + 2 * i) for i in range(2)]
        ksemb = psb("p9ksemb", (128, 32), BF16)
        rec = [self.at(f"p9rec{i}", (2, 1024), F32, 100 + 4 * i) for i in range(2)]

        s_wql = [sem(f"p9wql{i}", dma=True) for i in range(2)]
        s_qtf = sem("p9qtf")
        s_qev = sem("p9qev")
        s_ke = sem("p9ke")
        s_nmm = sem("p9nmm")
        s_nrc = sem("p9nrc")
        s_bcm = sem("p9bcm")
        s_qsc = sem("p9qsc")

        @blk.gpsimd
        def _(g):
            self.s_p8_pmm.wait(g, 16)
            for u in range(32):
                cb, kcg = u // 4, u % 4
                if u >= 2:
                    s_qtf.wait(g, u - 1)
                s_wql[u % 2].inc(g.dma_start(
                    out=wqb[u % 2][:, :],
                    in_=d["wq1"][cb, kcg].rearrange("p k c -> p (k c)")), 16)

        ps_q = self.ps_q1

        @blk.tensor
        def _(t):
            self.s_p8_pevK.wait(t, 8)
            self.s_p8_pevV.wait(t, 8)
            for cb in range(8):
                if cb >= 1:
                    s_qev.wait(t, 2 * cb)
                for kcg in range(4):
                    u = cb * 4 + kcg
                    s_wql[u % 2].wait(t, 16 * (u // 2 + 1))
                    for kin in range(4):
                        kc = kcg * 4 + kin
                        for mcl in range(2):
                            lhsT = wqb[u % 2][:, kin * 256 + mcl * 128:
                                              kin * 256 + (mcl + 1) * 128]
                            for ts in range(2):
                                i_ = self.mm(t, ps_q[mcl][:, ts * 512:(ts + 1) * 512],
                                             lhsT,
                                             self.xn1[:, kc * 1024 + ts * 512:
                                                      kc * 1024 + (ts + 1) * 512],
                                             kc == 0, kc == 15, ldw=(ts == 0))
                                if kin == 3 and mcl == 1 and ts == 1:
                                    s_qtf.inc(i_)

        @blk.scalar
        def _(s):
            s.drain()
            self.s_p8_pmm.wait(s, 16)
            for cb in range(8):
                s_qtf.wait(s, 4 * (cb + 1))
                for mcl in range(2):
                    m = cb * 2 + mcl
                    if m:
                        s.drain()
                    s_qev.inc(s.activation(q1[:, m * 1024:(m + 1) * 1024],
                                           ps_q[mcl][:, :], AF.Relu))

        self.stQ.close()
        ps_n = [ph.enter_context(nc.psum_tensor(f"ps_n{i}", [2, 1024], F32))
                for i in range(2)]
        ps_bc = [ph.enter_context(nc.psum_tensor(f"ps_bc{i}", [128, 1024], F32))
                 for i in range(2)]

        @blk.vector
        def _(v):
            self.s_p8_cast.wait(v, self.s_p8_cast.n)
            v.drain()
            v.memset(ksemb[:, :], 0.0)
            v.drain()
            last = None
            for c in range(16):
                v.tensor_copy(ksemb[0:64, 2 * c:2 * c + 1], self.ksf[0:64, c:c + 1])
                last = v.tensor_copy(ksemb[64:128, 2 * c + 1:2 * c + 2],
                                     self.ksf[64:128, c:c + 1])
            s_ke.inc(last)
            for c in range(16):
                sl = c % 2
                s_bcm.wait(v, c + 1)
                v.drain()
                v.tensor_tensor(q1[:, c * 1024:c * 1024 + 512],
                                q1[:, c * 1024:c * 1024 + 512],
                                ps_bc[sl][:, 0:512], ALU.mult)
                s_qsc.inc(v.tensor_tensor(q1[:, c * 1024 + 512:(c + 1) * 1024],
                                          q1[:, c * 1024 + 512:(c + 1) * 1024],
                                          ps_bc[sl][:, 512:1024], ALU.mult))

        def bcast(t, cb):
            sl = cb % 2
            s_nrc.wait(t, cb + 1)
            if cb >= 2:
                s_qsc.wait(t, cb - 1)
            t.matmul(ps_bc[sl][:, 0:512], self.e2F_sb[:, :],
                     rec[sl][:, 0:512], start=True, stop=True)
            s_bcm.inc(t.matmul(ps_bc[sl][:, 512:1024], self.e2F_sb[:, :],
                               rec[sl][:, 512:1024], start=True, stop=True))

        @blk.tensor
        def _(t):
            s_qev.wait(t, 16)
            s_ke.wait(t, 1)
            for c in range(16):
                sl = c % 2
                if c >= 2:
                    s_nrc.wait(t, c - 1)
                for ts in range(2):
                    i_ = self.mm(t, ps_n[sl][:, ts * 512:(ts + 1) * 512],
                                 ksemb[:, 2 * c:2 * (c + 1)],
                                 q1[:, c * 1024 + ts * 512: c * 1024 + (ts + 1) * 512],
                                 True, True, ldw=(ts == 0))
                    if ts == 1:
                        s_nmm.inc(i_)
                if c >= 2:
                    bcast(t, c - 2)
            bcast(t, 14)
            bcast(t, 15)

        @blk.scalar
        def _(s):
            for c in range(16):
                sl = c % 2
                s_nmm.wait(s, c + 1)
                if c >= 2:
                    s_bcm.wait(s, c - 1)
                s.drain()
                s.activation(rec[sl][:, :], ps_n[sl][:, :], AF.Ln,
                             bias=self.const_sb[0:2, 1:2])
                s.drain()
                s_nrc.inc(s.activation(rec[sl][:, :], rec[sl][:, :], AF.Exp,
                                       scale=-1.0))

        self.s_p9_qev = s_qev
        self.s_p9_qsc = s_qsc
        self.s_p9_done = s_qsc
        ph.close()


    def p10(self):
        self.phase()
        blk, d, nc = self.blk, self.d, self.nc
        sem = self.sem
        ph = ExitStack()
        psb = lambda n, s, dt=F32: self.rsb(ph, n, s, dt)

        at1 = self.at("at1", (128, 16 * 1024), BF16, 96)
        wob = [psb(f"p10wo{i}", (128, 1024), BF16) for i in range(2)]
        s1b = [psb(f"p10s1{i}", (128, 1024), BF16) for i in range(2)]

        s_atm = sem("p10atm")
        s_ate = sem("p10ate")
        s_wol = [sem(f"p10wol{i}", dma=True) for i in range(2)]
        s_wtf = sem("p10wtf")
        s_wev = sem("p10wev")
        s_sp = [sem(f"p10sp{i}", dma=True) for i in range(2)]
        self.s_p10_sp = s_sp

        phA = ExitStack()
        ps_at = [phA.enter_context(nc.psum_tensor(f"ps_at{i}", [128, 512], F32))
                 for i in range(2)]

        @blk.gpsimd
        def _(g):
            self.s_p9_qsc.wait(g, 16)
            for u in range(32):
                cb, kcg = u // 4, u % 4
                if u >= 2:
                    s_wtf.wait(g, u - 1)
                s_wol[u % 2].inc(g.dma_start(
                    out=wob[u % 2][:, :],
                    in_=d["wo1"][cb, kcg].rearrange("p k c -> p (k c)")), 16)

        @blk.tensor
        def _(t):
            for u in range(32):
                pair, ts = u // 2, u % 2
                sl = u % 2
                self.s_p9_qsc.wait(t, pair + 1)
                if u >= 2:
                    s_ate.wait(t, u - 1)
                kvt = self.kvbf[:, pair * 128:(pair + 1) * 128]
                qsl = self.q1_ref[:, pair * 1024 + ts * 512: pair * 1024 + (ts + 1) * 512]
                self.mm(t, ps_at[sl][0:64, :], kvt[0:64, 0:64], qsl[0:64, :],
                        True, True)
                i_ = self.mm(t, ps_at[sl][64:128, :], kvt[64:128, 64:128],
                             qsl[64:128, :], True, True)
                s_atm.inc(i_)

        @blk.vector
        def _(v):
            v.drain()
            for u in range(32):
                pair, ts = u // 2, u % 2
                sl = u % 2
                s_atm.wait(v, u + 1)
                s_ate.inc(v.tensor_copy(
                    at1[:, pair * 1024 + ts * 512: pair * 1024 + (ts + 1) * 512],
                    ps_at[sl][:, :]))

        phA.close()
        phB = ExitStack()
        ps_w = [phB.enter_context(nc.psum_tensor(f"ps_w1{i}", [128, 1024], F32))
                for i in range(2)]

        @blk.tensor
        def _(t):
            s_ate.wait(t, 32)
            for cb in range(8):
                if cb >= 1:
                    s_wev.wait(t, 2 * cb)
                for kcg in range(4):
                    u = cb * 4 + kcg
                    s_wol[u % 2].wait(t, 16 * (u // 2 + 1))
                    for kin in range(4):
                        kc = kcg * 4 + kin
                        for mcl in range(2):
                            lhsT = wob[u % 2][:, kin * 256 + mcl * 128:
                                              kin * 256 + (mcl + 1) * 128]
                            for ts in range(2):
                                i_ = self.mm(t, ps_w[mcl][:, ts * 512:(ts + 1) * 512],
                                             lhsT,
                                             at1[:, kc * 1024 + ts * 512:
                                                 kc * 1024 + (ts + 1) * 512],
                                             kc == 0, kc == 15, ldw=(ts == 0))
                                if kin == 3 and mcl == 1 and ts == 1:
                                    s_wtf.inc(i_)

        @blk.vector
        def _(v):
            v.drain()
            for cb in range(8):
                s_wtf.wait(v, 4 * (cb + 1))
                for mcl in range(2):
                    m = cb * 2 + mcl
                    if m >= 2:
                        s_sp[m % 2].wait(v, 16 * (m // 2))
                    if m:
                        v.drain()
                    s_wev.inc(v.tensor_scalar(s1b[m % 2][:, :], ps_w[mcl][:, :],
                                              self.bo1_sb[:, m:m + 1], None, ALU.add))

        @blk.sync
        def _(sy):
            for m in range(16):
                s_wev.wait(sy, m + 1)
                s_sp[m % 2].inc(sy.dma_start(out=self.s1_s[m],
                                             in_=s1b[m % 2][:, :]), 16)

        phB.close()
        self.s_p10_wev = s_wev
        self.s_p10_wtf = s_wtf
        self.s_p10_done = s_wev
        ph.close()
        self.st8L.close()

    def p11(self):
        self.phase()
        blk, d, nc = self.blk, self.d, self.nc
        sem = self.sem
        ph = ExitStack()
        psb = lambda n, s, dt=F32: self.rsb(ph, n, s, dt)

        hdn = self.at("hdn1", (128, NI1 * 1024), BF16, 32)
        gbuf = [psb(f"p11g{i}", (128, 1024), BF16) for i in range(2)]
        ubuf = [psb(f"p11u{i}", (128, 1024), BF16) for i in range(2)]
        t1 = psb("p11t1", (128, 2048))
        tmp = [psb(f"p11tmp{i}", (128, 512), BF16) for i in range(7)]

        s_gl = [sem(f"p11gl{i}", dma=True) for i in range(2)]
        s_ul = [sem(f"p11ul{i}", dma=True) for i in range(2)]
        s_gtf = sem("p11gtf")
        s_utf = sem("p11utf")
        s_sil = sem("p11sil")
        s_hdn = sem("p11hdn")
        s_thr = sem("p11thr")

        gateA = self.s_p10_wtf
        GA = gateA.n

        @blk.gpsimd
        def _(g):
            gateA.wait(g, GA)
            for cb in range(32):
                for kcg in range(4):
                    u = cb * 4 + kcg
                    if u >= 2:
                        s_gtf.wait(g, u - 1)
                    s_gl[u % 2].inc(g.dma_start(
                        out=gbuf[u % 2][:, :],
                        in_=d["wgate1"][cb, kcg].rearrange("p k c -> p (k c)")), 16)
                for kcg in range(4):
                    u = cb * 4 + kcg
                    if u >= 2:
                        s_utf.wait(g, u - 1)
                    s_ul[u % 2].inc(g.dma_start(
                        out=ubuf[u % 2][:, :],
                        in_=d["wup1"][cb, kcg].rearrange("p k c -> p (k c)")), 16)

        phA = ExitStack()
        ps_g = [phA.enter_context(nc.psum_tensor(f"ps_g1{i}", [128, 1024], F32))
                for i in range(2)]
        ps_u = [phA.enter_context(nc.psum_tensor(f"ps_u1{i}", [128, 1024], F32))
                for i in range(2)]

        @blk.tensor
        def _(t):
            self.s_p10_wev.wait(t, 16)
            for cb in range(32):
                if cb >= 1:
                    s_sil.wait(t, 4 * cb)
                for kcg in range(4):
                    u = cb * 4 + kcg
                    s_gl[u % 2].wait(t, 16 * (u // 2 + 1))
                    for kin in range(4):
                        kc = kcg * 4 + kin
                        for icl in range(2):
                            lhsT = gbuf[u % 2][:, kin * 256 + icl * 128:
                                               kin * 256 + (icl + 1) * 128]
                            for ts in range(2):
                                i_ = self.mm(t, ps_g[icl][:, ts * 512:(ts + 1) * 512],
                                             lhsT,
                                             self.xn1[:, kc * 1024 + ts * 512:
                                                      kc * 1024 + (ts + 1) * 512],
                                             kc == 0, kc == 15, ldw=(ts == 0))
                                if kin == 3 and icl == 1 and ts == 1:
                                    s_gtf.inc(i_)
                if cb >= 1:
                    s_hdn.wait(t, 4 * cb)
                for kcg in range(4):
                    u = cb * 4 + kcg
                    s_ul[u % 2].wait(t, 16 * (u // 2 + 1))
                    for kin in range(4):
                        kc = kcg * 4 + kin
                        for icl in range(2):
                            lhsT = ubuf[u % 2][:, kin * 256 + icl * 128:
                                               kin * 256 + (icl + 1) * 128]
                            for ts in range(2):
                                i_ = self.mm(t, ps_u[icl][:, ts * 512:(ts + 1) * 512],
                                             lhsT,
                                             self.xn1[:, kc * 1024 + ts * 512:
                                                      kc * 1024 + (ts + 1) * 512],
                                             kc == 0, kc == 15, ldw=(ts == 0))
                                if kin == 3 and icl == 1 and ts == 1:
                                    s_utf.inc(i_)

        @blk.scalar
        def _(s):
            s.drain()
            self.s_p10_sp[0].wait(s, 128)
            self.s_p10_sp[1].wait(s, 128)
            for cb in range(32):
                s_gtf.wait(s, 4 * (cb + 1))
                if cb >= 1:
                    s_hdn.wait(s, 4 * cb)
                for icl in range(2):
                    for ts in range(2):
                        if cb + icl + ts:
                            s.drain()
                        s_sil.inc(s.activation(
                            t1[:, (icl * 2 + ts) * 512:(icl * 2 + ts + 1) * 512],
                            ps_g[icl][:, ts * 512:(ts + 1) * 512], AF.Silu))

        @blk.vector
        def _(v):
            v.drain()
            for cb in range(32):
                v_qrt = cb // 2
                s_utf.wait(v, 4 * (cb + 1))
                s_sil.wait(v, 4 * (cb + 1))
                for icl in range(2):
                    ic = cb * 2 + icl
                    for ts in range(2):
                        s_hdn.inc(v.tensor_tensor(
                            hdn[:, ic * 1024 + ts * 512: ic * 1024 + (ts + 1) * 512],
                            t1[:, (icl * 2 + ts) * 512:(icl * 2 + ts + 1) * 512],
                            ps_u[icl][:, ts * 512:(ts + 1) * 512], ALU.mult))
                if cb % 2 == 1:
                    q4 = v_qrt * 4
                    v.drain()
                    for ts in range(2):
                        hsl = [hdn[:, (q4 + j) * 1024 + ts * 512:
                                   (q4 + j) * 1024 + (ts + 1) * 512] for j in range(4)]
                        S = [tmp[j][:, :] for j in range(4)]
                        T1, T2, T3 = tmp[4][:, :], tmp[5][:, :], tmp[6][:, :]
                        for j in range(4):
                            v.tensor_mul(S[j], hsl[j], hsl[j])
                        v.drain()
                        v.tensor_max(T1, S[0], S[1])
                        v.tensor_max(T2, S[2], S[3])
                        v.tensor_tensor(T3, S[0], S[1], ALU.min)
                        v.drain()
                        v.tensor_tensor(T1, T1, T2, ALU.min)
                        v.drain()
                        v.tensor_tensor(T2, S[2], S[3], ALU.min)
                        v.drain()
                        v.tensor_max(T2, T2, T3)
                        v.drain()
                        v.tensor_max(T1, T1, T2)
                        v.drain()
                        for j in range(4):
                            v.tensor_tensor(T2, S[j], T1, ALU.is_ge)
                            v.drain()
                            v.tensor_mul(hsl[j], hsl[j], T2)
                            v.drain()
                    s_thr.inc(v.drain())

        phA.close()

        wdbuf = [self.at(f"p11wd{i}", (128, 4 * 512), BF16, 0 + 4 * i) for i in range(2)]
        t2 = [self.at(f"p11t2{i}", (128, 512), F32, 8 + 2 * i) for i in range(2)]
        s1l = [self.at(f"p11s1{i}", (128, 1024), BF16, 12 + 2 * i) for i in range(2)]
        x1l = [self.at(f"p11x1{i}", (128, 1024), F32, 16 + 4 * i) for i in range(2)]
        outb = [self.at(f"p11ob{i}", (128, 1024), F32, 24 + 4 * i) for i in range(2)]
        ps_d = [ph.enter_context(nc.psum_tensor(f"ps_d1{i}", [128, 512], F32))
                for i in range(8)]

        s_wdl = [sem(f"p11wdl{i}", dma=True) for i in range(2)]
        s_dwf = sem("p11dwf")
        s_dsil = sem("p11dsil")
        s_fus = sem("p11fus")
        s_s1l = [sem(f"p11s1l{i}", dma=True) for i in range(2)]
        s_x1l = [sem(f"p11x1l{i}", dma=True) for i in range(2)]
        s_out = [self.sem(f"p11out{i}", keep=True) for i in range(2)]
        self.final_sems += s_out

        @blk.gpsimd
        def _(g):
            for u in range(64):
                sw, icg = u // 16, u % 16
                if u < 2:
                    s_gtf.wait(g, 128)
                    s_utf.wait(g, 128)
                else:
                    s_dwf.wait(g, u - 1)
                s_wdl[u % 2].inc(g.dma_start(
                    out=wdbuf[u % 2][:, :],
                    in_=d["wdown1"][sw, icg].rearrange("p k c -> p (k c)")), 16)

        @blk.tensor
        def _(t):
            s_thr.wait(t, 16)
            for sw in range(4):
                if sw >= 1:
                    s_dsil.wait(t, 8 * sw)
                for icg in range(16):
                    u = sw * 16 + icg
                    s_wdl[u % 2].wait(t, 16 * (u // 2 + 1))
                    for kin in range(4):
                        ic = icg * 4 + kin
                        for ocl in range(4):
                            lhsT = wdbuf[u % 2][:, kin * 512 + ocl * 128:
                                                kin * 512 + (ocl + 1) * 128]
                            for ts in range(2):
                                i_ = self.mm(t, ps_d[ocl * 2 + ts][:, :], lhsT,
                                             hdn[:, ic * 1024 + ts * 512:
                                                 ic * 1024 + (ts + 1) * 512],
                                             ic == 0, ic == 63, ldw=(ts == 0))
                                if kin == 3 and ocl == 3 and ts == 1:
                                    s_dwf.inc(i_)

        @blk.scalar
        def _(s):
            for u in range(32):
                sw = u // 8
                ocl, ts = (u % 8) // 2, u % 2
                sl = u % 2
                s_dwf.wait(s, 16 * (sw + 1))
                if u >= 2:
                    s_fus.wait(s, u - 1)
                s.drain()
                s_dsil.inc(s.activation(t2[sl][:, :], ps_d[ocl * 2 + ts][:, :],
                                        AF.Silu))

        @blk.vector
        def _(v):
            for u in range(32):
                sw = u // 8
                ocl, ts = (u % 8) // 2, u % 2
                oc = sw * 4 + ocl
                sl = u % 2
                s_dsil.wait(v, u + 1)
                s_s1l[oc % 2].wait(v, 16 * (oc // 2 + 1))
                s_x1l[oc % 2].wait(v, 16 * (oc // 2 + 1))
                v.drain()
                v.tensor_tensor(t2[sl][:, :], t2[sl][:, :],
                                self.s1_ldview(s1l, oc, ts), ALU.mult)
                v.drain()
                if u >= 4:
                    s_out[oc % 2].wait(v, 16 * (oc // 2))
                s_fus.inc(v.tensor_add(
                    outb[oc % 2][:, ts * 512:(ts + 1) * 512], t2[sl][:, :],
                    x1l[oc % 2][:, ts * 512:(ts + 1) * 512]))

        @blk.sync
        def _(sy):
            for oc in range(16):
                if oc < 2:
                    s_gtf.wait(sy, 128)
                    s_utf.wait(sy, 128)
                else:
                    sy_w = s_fus
                    sy_w.wait(sy, 2 * (oc - 1))
                s_s1l[oc % 2].inc(sy.dma_start(out=s1l[oc % 2][:, :],
                                               in_=self.s1_s[oc]), 16)
                s_x1l[oc % 2].inc(sy.dma_start(out=x1l[oc % 2][:, :],
                                               in_=self.x1_s[oc]), 16)
                if oc >= 1:
                    s_fus.wait(sy, 2 * oc)
                    s_out[(oc - 1) % 2].inc(sy.dma_start(
                        out=self.out_d[oc - 1], in_=outb[(oc - 1) % 2][:, :]), 16)
            s_fus.wait(sy, 32)
            s_out[1].inc(sy.dma_start(out=self.out_d[15], in_=outb[1][:, :]), 16)

    def s1_ldview(self, s1l, oc, ts):
        return s1l[oc % 2][:, ts * 512:(ts + 1) * 512]

    def finish(self):
        fs = self.final_sems

        @self.blk.sync
        def _(sy):
            for s_ in fs:
                s_.wait(sy, s_.n)

    def build(self):
        self.declare_io()
        self.alloc_persistent()
        self.blk = self.st.enter_context(self.nc.Block())
        self.p0()
        self.p1()
        self.p2()
        self.p3()
        self.p4()
        self.p5()
        self.p6()
        self.p7()
        self.p8()
        self.p9()
        self.p10()
        self.p11()
        self.finish()
        self.st.close()
        return self.nc


def build_program(dbg=()):
    return K(dbg).build()


def kernel(**inputs):
    in_maps = stage_inputs(inputs)
    nc = build_program()
    res = run_bass_kernel_spmd(nc, in_maps, core_ids=list(range(8)))
    out = np.empty((B, S, H), dtype=np.float32)
    for c in range(8):
        seq, half = c // 2, c % 2
        o = res.results[c]["out"]
        out[seq, half * T:(half + 1) * T, :] = o.reshape(H, T).T
    return out
